# version 22
# baseline (speedup 1.0000x reference)
import numpy as np
import ml_dtypes
import concourse.bass as bass
import concourse.mybir as mybir
from concourse.bass_utils import run_bass_kernel_spmd
from contextlib import ExitStack

F32 = mybir.dt.float32
BF16 = mybir.dt.bfloat16
AF = mybir.ActivationFunctionType
ALU = mybir.AluOpType
AX = mybir.AxisListType

NT = 16
D = 1024
N = 2048
NC = 8
EPS = 1e-5
ENGS = ("t", "v", "s", "g", "y")


def bcast0(ap, dim, count):
    dims = [list(d) for d in ap.ap]
    dims[dim] = [0, count]
    return bass.AP(ap.tensor, ap.offset, dims)


class Plan:
    def __init__(self, nc, es):
        self.nc = nc
        self.es = es
        self.steps = []
        self.sem = {}
        self.val = {}
        self.dirty = {e: False for e in ENGS}
        self.last_step = {e: None for e in ENGS}
        self.last_snap = []

    def new_sem(self, name):
        if name in self.sem:
            return
        self.sem[name] = self.es.enter_context(self.nc.semaphore(name))
        self.val[name] = 0

    def st(self, eng, fn=None, wait=(), inc=()):
        waits = []
        for w in wait:
            if isinstance(w, tuple):
                if w[1] > 0:
                    waits.append(w)
            else:
                if self.val[w] > 0:
                    waits.append((w, self.val[w]))
        incs = []
        for name, n in inc:
            self.val[name] += n
            incs.append((name, n))
        step = dict(eng=eng, fn=fn, waits=waits, incs=incs,
                    post={name: self.val[name] for (name, n) in incs})
        self.steps.append(step)
        if fn is not None:
            self.dirty[eng] = True
            self.last_step[eng] = step
        return step

    def mark(self, eng, sname, n=1):
        key = sname + "_" + eng
        self.new_sem(key)
        stp = self.last_step[eng]
        assert stp is not None and stp["fn"] is not None, f"mark {eng} {sname}"
        self.val[key] += n
        stp["incs"].append((key, n))
        stp["post"][key] = self.val[key]
        return (key, self.val[key])

    def cur(self, key):
        return (key, self.val.get(key, 0))

    def barrier(self, engs=("t", "v", "s", "g")):
        snap = []
        for e in engs:
            if not self.dirty[e]:
                continue
            stp = self.last_step[e]
            if stp["incs"]:
                sname, _ = stp["incs"][-1]
                snap.append((e, sname, stp["post"][sname]))
            else:
                key, v = self.mark(e, "B", 16 if e in ("g", "y") else 1)
                snap.append((e, key, v))
            self.dirty[e] = False
        for e in engs:
            waits = [(s, v) for (eo, s, v) in snap if v > 0]
            if waits:
                self.st(e, None, wait=waits)
        self.last_snap = [(s, v) for (_, s, v) in snap]
        return self.last_snap


def build_program():
    nc = bass.Bass("TRN2", target_bir_lowering=False, debug=False)

    dx = nc.dram_tensor("x", [N, D], F32, kind="ExternalInput").ap()
    dy = nc.dram_tensor("y", [N, D], F32, kind="ExternalOutput").ap()
    d_wq = nc.dram_tensor("wq", [D, D], F32, kind="ExternalInput").ap()
    d_wk = nc.dram_tensor("wk", [D, 256], F32, kind="ExternalInput").ap()
    d_wv = nc.dram_tensor("wv", [D, 256], F32, kind="ExternalInput").ap()
    d_wo = nc.dram_tensor("wo", [D, D], F32, kind="ExternalInput").ap()
    d_bq = nc.dram_tensor("bq", [D], F32, kind="ExternalInput").ap()
    d_bk = nc.dram_tensor("bk", [256], F32, kind="ExternalInput").ap()
    d_bv = nc.dram_tensor("bv", [256], F32, kind="ExternalInput").ap()
    d_bo = nc.dram_tensor("bo", [1, D], F32, kind="ExternalInput").ap()
    d_lng = nc.dram_tensor("lng", [D], F32, kind="ExternalInput").ap()
    d_betf = nc.dram_tensor("betf", [D], F32, kind="ExternalInput").ap()
    d_nbet = nc.dram_tensor("nbet", [1, D], F32, kind="ExternalInput").ap()
    d_sigsT = nc.dram_tensor("sigsT", [D, 64], F32, kind="ExternalInput").ap()
    d_knots = nc.dram_tensor("knots", [1, 8], F32, kind="ExternalInput").ap()
    d_temp = nc.dram_tensor("temp", [1, 1], F32, kind="ExternalInput").ap()
    d_cw1 = nc.dram_tensor("cw1", [D, 256], F32, kind="ExternalInput").ap()
    d_cb1 = nc.dram_tensor("cb1", [256], F32, kind="ExternalInput").ap()
    d_cw2 = nc.dram_tensor("cw2", [256, 2], F32, kind="ExternalInput").ap()
    d_cb2 = nc.dram_tensor("cb2", [2], F32, kind="ExternalInput").ap()
    d_w1f = nc.dram_tensor("w1f", [2, D], F32, kind="ExternalInput").ap()
    d_b1f = nc.dram_tensor("b1f", [D], F32, kind="ExternalInput").ap()
    d_w2bd = nc.dram_tensor("w2bd", [D, 64], F32, kind="ExternalInput").ap()
    d_b2sp = nc.dram_tensor("b2sp", [64], F32, kind="ExternalInput").ap()
    d_dirs = nc.dram_tensor("dirs", [64, D], F32, kind="ExternalInput").ap()
    d_dirsT = nc.dram_tensor("dirsT", [D, 64], F32, kind="ExternalInput").ap()
    d_gam = nc.dram_tensor("gam", [1, D], F32, kind="ExternalInput").ap()
    d_osc = nc.dram_tensor("osc", [1, 1], F32, kind="ExternalInput").ap()
    d_idf = nc.dram_tensor("idf", [128, 128], F32, kind="ExternalInput").ap()
    d_idb = nc.dram_tensor("idb", [128, 128], BF16, kind="ExternalInput").ap()
    d_tri = nc.dram_tensor("tri", [128, 2, 128], BF16, kind="ExternalInput").ap()
    d_negi = nc.dram_tensor("negi", [128, 128], BF16, kind="ExternalInput").ap()
    d_tril = nc.dram_tensor("tril", [128, 128], BF16, kind="ExternalInput").ap()
    d_prio = nc.dram_tensor("prio", [128, 8], F32, kind="ExternalInput").ap()
    d_onesf = nc.dram_tensor("onesf", [1, 128], F32, kind="ExternalInput").ap()
    d_onesqb = nc.dram_tensor("onesqb", [128, 128], BF16, kind="ExternalInput").ap()

    es = ExitStack()
    P = Plan(nc, es)
    for s in ["LD", "LDG", "ST"]:
        P.new_sem(s)

    _uid = [0]

    def sbuf(stack, name, shape, dt, side=None):
        _uid[0] += 1
        return stack.enter_context(nc.sbuf_tensor(f"{name}_{_uid[0]}", shape, dt, side=side))

    def psum(stack, name, shape, dt):
        _uid[0] += 1
        return stack.enter_context(nc.psum_tensor(f"{name}_{_uid[0]}", shape, dt))

    sX = es.enter_context(nc.sbuf_tensor("sX", [128, NT, D], F32))
    sT = es.enter_context(nc.sbuf_tensor("sT", [128, NC, N], BF16))
    sIDF = es.enter_context(nc.sbuf_tensor("sIDF", [128, 128], F32))
    sIDB = es.enter_context(nc.sbuf_tensor("sIDB", [128, 128], BF16))
    sTRI = es.enter_context(nc.sbuf_tensor("sTRI", [128, 2, 128], BF16))
    sNEGI = es.enter_context(nc.sbuf_tensor("sNEGI", [128, 128], BF16))
    sTRIL = es.enter_context(nc.sbuf_tensor("sTRIL", [128, 128], BF16))
    sPRIO = es.enter_context(nc.sbuf_tensor("sPRIO", [128, 8], F32))
    sONESF = es.enter_context(nc.sbuf_tensor("sONESF", [1, 128], F32))
    sONESQB = es.enter_context(nc.sbuf_tensor("sONESQB", [128, 128], BF16))
    sROWS = es.enter_context(nc.sbuf_tensor("sROWS", [1, 16], F32))
    sKN = es.enter_context(nc.sbuf_tensor("sKN", [128, 36], F32))
    sGB = es.enter_context(nc.sbuf_tensor("sGB", [128, 2, D], BF16))
    sSTAT = es.enter_context(nc.sbuf_tensor("sSTAT", [128, NT, 12], F32))
    sMV4 = es.enter_context(nc.sbuf_tensor("sMV4", [128, NT, 4], F32))
    sRS = es.enter_context(nc.sbuf_tensor("sRS", [128, NT, 2], F32))
    sRS2 = es.enter_context(nc.sbuf_tensor("sRS2", [128, NT, 2], F32))
    sSCR = es.enter_context(nc.sbuf_tensor("sSCR", [128, NT, 4], F32))
    sBIAS = es.enter_context(nc.sbuf_tensor("sBIAS", [128, 32], F32))
    sCB2 = es.enter_context(nc.sbuf_tensor("sCB2", [2, 1], F32))
    sB2SP = es.enter_context(nc.sbuf_tensor("sB2SP", [64, 1], F32))
    sXFB = es.enter_context(nc.sbuf_tensor("sXFB", [128, 3, D], BF16))
    sGC = es.enter_context(nc.sbuf_tensor("sGC", [128, 8], F32))
    sSG = es.enter_context(nc.sbuf_tensor("sSG", [128, 8, 72], BF16))
    sDSB = es.enter_context(nc.sbuf_tensor("sDSB", [128, 2, 64], F32))
    sDG = es.enter_context(nc.sbuf_tensor("sDG", [128, NC, 128], BF16))
    sGBr = es.enter_context(nc.sbuf_tensor("sGBr", [2, D], BF16))

    def dma(q, out, in_, sem=None, **kw):
        if sem is None:
            sem = "LD" if q == "y" else "LDG"
        P.new_sem(sem)
        P.st(q, lambda e, o=out, i=in_, kw=kw: e.dma_start(out=o, in_=i, **kw),
             wait=list(P.last_snap), inc=[(sem, 16)])
        return P.val[sem]

    setup_es = ExitStack()
    sGRow = sbuf(setup_es, "sGRow", [1, 2, D], F32)
    pKN = psum(setup_es, "pKN", [128, 16], F32)
    pGB = psum(setup_es, "pGB", [128, D], F32)

    dma("y", sIDF[:], d_idf)
    dma("y", sIDB[:], d_idb)
    dma("y", sTRI[:], d_tri)
    dma("y", sNEGI[:], d_negi)
    dma("y", sTRIL[:], d_tril)
    dma("y", sPRIO[:], d_prio)
    dma("y", sONESF[:], d_onesf)
    dma("y", sONESQB[:], d_onesqb)
    dma("y", sROWS[0:1, 0:8], d_knots)
    dma("y", sROWS[0:1, 8:9], d_temp)
    dma("y", sROWS[0:1, 9:10], d_osc)
    dma("y", sGRow[0:1, 0, :], d_gam)
    dma("y", sGRow[0:1, 1, :], d_nbet)
    dma("y", sBIAS[:, 0:8], d_bq.rearrange("(c p) -> p c", p=128), allow_slow_non_contiguous=True)
    dma("y", sBIAS[:, 8:10], d_bk.rearrange("(c p) -> p c", p=128), allow_slow_non_contiguous=True)
    dma("y", sBIAS[:, 10:12], d_bv.rearrange("(c p) -> p c", p=128), allow_slow_non_contiguous=True)
    dma("y", sBIAS[:, 12:14], d_cb1.rearrange("(c p) -> p c", p=128), allow_slow_non_contiguous=True)
    dma("y", sBIAS[:, 14:22], d_b1f.rearrange("(c p) -> p c", p=128), allow_slow_non_contiguous=True)
    dma("y", sBIAS[:, 22:30], d_lng.rearrange("(c p) -> p c", p=128), allow_slow_non_contiguous=True)
    dma("y", sCB2[:], d_cb2.rearrange("(a b) -> a b", b=1))
    dma("y", sB2SP[:], d_b2sp.rearrange("(a b) -> a b", b=1))
    dma("y", sGC[:], d_gam.rearrange("o (c p) -> p (o c)", p=128),
        allow_slow_non_contiguous=True)
    LD_setup = P.val["LD"]
    xld = []
    for q4 in range(4):
        dma("y", sX[:, 4 * q4:4 * (q4 + 1), :],
            dx.rearrange("(i p) f -> p i f", p=128)[:, 4 * q4:4 * (q4 + 1), :],
            sem=f"LDX{q4}")
        xld.append((f"LDX{q4}", 16))


    def setup_small(v):
        v.tensor_scalar(sBIAS[:, 0:8], sBIAS[:, 0:8], 0.125, None, ALU.mult)
        v.tensor_scalar(sROWS[0:1, 10:11], sROWS[0:1, 8:9], 1e-6, None, ALU.add)
        return v.reciprocal(sROWS[0:1, 11:12], sROWS[0:1, 10:11])
    P.st("v", setup_small, wait=[("LD", LD_setup)])
    a_small = P.mark("v", "AUX")

    def mm_kn(t):
        t.matmul(pKN[:, 0:8], sONESF[:], sROWS[0:1, 0:8], start=True, stop=True)
        t.matmul(pKN[:, 8:9], sONESF[:], sROWS[0:1, 11:12], start=True, stop=True)
        return t.matmul(pKN[:, 9:10], sONESF[:], sROWS[0:1, 9:10], start=True, stop=True)
    P.st("t", mm_kn, wait=[a_small, ("LD", LD_setup)])
    a_kn = P.mark("t", "AUX")

    def kn_evac(v):
        v.tensor_copy(sKN[:, 0:8], pKN[:, 0:8])
        v.tensor_scalar(sKN[:, 16:17], pKN[:, 8:9], -1.0, None, ALU.mult)
        v.tensor_copy(sKN[:, 17:18], pKN[:, 9:10])
        v.tensor_tensor(sKN[:, 8:15], sKN[:, 1:8], sKN[:, 0:7], ALU.subtract)
        v.tensor_copy(sKN[:, 20:21], sKN[:, 8:9])
        v.tensor_tensor(sKN[:, 21:27], sKN[:, 9:15], sKN[:, 8:14], ALU.subtract)
        return v.tensor_scalar(sKN[:, 28:35], sPRIO[:, 0:7], -8.0, None, ALU.add)
    P.st("v", kn_evac, wait=[a_kn])
    P.mark("v", "AUX")

    def mk_gb(idx):
        def f(t):
            mi = None
            for half in range(2):
                mi = t.matmul(pGB[:, 512 * half:512 * (half + 1)], sONESF[:],
                              sGRow[0:1, idx, 512 * half:512 * (half + 1)],
                              start=True, stop=True)
            return mi
        return f
    P.st("t", mk_gb(0), wait=[("LD", LD_setup)])
    a_g0 = P.mark("t", "AUX")
    P.st("s", lambda s: s.activation(sGB[:, 0, :], pGB[:], AF.Copy, scale=0.5),
         wait=[a_g0])
    a_g1 = P.mark("s", "AUX")
    P.st("t", mk_gb(1), wait=[a_g1])
    a_g2 = P.mark("t", "AUX")
    P.st("s", lambda s: s.activation(sGB[:, 1, :], pGB[:], AF.Copy, scale=0.5),
         wait=[a_g2])
    P.mark("s", "AUX")

    sGCB = sbuf(setup_es, "sGCB", [128, 8], BF16)
    def mk_dg(v):
        v.tensor_scalar(sGCB[:], sGC[:], 0.5, None, ALU.mult)
        mi = None
        for c in range(NC):
            mi = v.tensor_tensor(sDG[:, c, :], sIDB[:], bcast0(sGCB[:, c:c + 1], 1, 128),
                                 ALU.mult)
        return mi
    P.st("v", mk_dg, wait=[("LD", LD_setup)])
    P.mark("v", "AUX")
    P.barrier()
    dma("y", sGBr[0:1, :], sGB[0:1, 0, :], sem="LDGB")
    dma("y", sGBr[1:2, :], sGB[0:1, 1, :], sem="LDGB")
    ldgb = ("LDGB", P.val["LDGB"])
    setup_es.close()

    def emit_ln_stats(src_tiles, rs_out, tile_waits=None, quarters=False):
        if quarters:
            auxq = [None] * 4
            agq = []
            for q in range(4):
                for t in range(4 * q, 4 * q + 4):
                    ap = src_tiles(t)
                    tw = tile_waits[t] if tile_waits else []
                    P.st("v", lambda v, t=t, ap=ap:
                         v.bn_stats(sSTAT[:, t, 0:6], ap[:, 0:512]), wait=tw)
                    P.st("v", lambda v, t=t, ap=ap:
                         v.bn_stats(sSTAT[:, t, 6:12], ap[:, 512:1024]))
                sm = P.mark("v", "AUX")
                for i, t in enumerate(range(4 * q, 4 * q + 4)):
                    P.st("v", lambda v, t=t: v.bn_aggr(sMV4[:, t, 0:2], sSTAT[:, t, 0:6]),
                         wait=[sm] if i == 0 else [])
                    P.st("v", lambda v, t=t: v.bn_aggr(sMV4[:, t, 2:4], sSTAT[:, t, 6:12]))
                agq.append(P.mark("v", "AUX"))
                if q >= 1:
                    auxq[q - 1] = emit_ln_stats_tail(rs_out, [agq[q - 1]],
                                                     (4 * (q - 1), 4 * q))
            auxq[3] = emit_ln_stats_tail(rs_out, [agq[3]], (12, 16))
            return auxq
        for t in range(NT):
            ap = src_tiles(t)
            tw = tile_waits[t] if tile_waits else []
            P.st("v", lambda v, t=t, ap=ap: v.bn_stats(sSTAT[:, t, 0:6], ap[:, 0:512]), wait=tw)
            P.st("v", lambda v, t=t, ap=ap: v.bn_stats(sSTAT[:, t, 6:12], ap[:, 512:1024]))
        bn_mark = P.mark("v", "AUX")
        for t in range(NT):
            P.st("v", lambda v, t=t: v.bn_aggr(sMV4[:, t, 0:2], sSTAT[:, t, 0:6]),
                 wait=[bn_mark] if t == 0 else [])
            P.st("v", lambda v, t=t: v.bn_aggr(sMV4[:, t, 2:4], sSTAT[:, t, 6:12]))
        ag_mark = P.mark("v", "AUX")
        return emit_ln_stats_tail(rs_out, [ag_mark])

    def emit_ln_stats_tail(rs_out, wait0, ts=(0, NT)):
        t0, t1 = ts
        sMV4s = sMV4[:, t0:t1]
        sSCRs = sSCR[:, t0:t1]
        rs_out = rs_out[:, t0:t1]
        m1 = sMV4s[:, :, 0:1].rearrange("p t o -> p (t o)")
        v1 = sMV4s[:, :, 1:2].rearrange("p t o -> p (t o)")
        m2 = sMV4s[:, :, 2:3].rearrange("p t o -> p (t o)")
        v2 = sMV4s[:, :, 3:4].rearrange("p t o -> p (t o)")
        mu = sSCRs[:, :, 2:3].rearrange("p t o -> p (t o)")
        sc0 = sSCRs[:, :, 0:1].rearrange("p t o -> p (t o)")
        sc1 = sSCRs[:, :, 1:2].rearrange("p t o -> p (t o)")
        rs = rs_out[:, :, 0:1].rearrange("p t o -> p (t o)")
        nm = rs_out[:, :, 1:2].rearrange("p t o -> p (t o)")

        def stats_math(v):
            v.tensor_tensor(mu, m1, m2, ALU.add)
            v.tensor_scalar(mu, mu, 0.5, None, ALU.mult)
            v.tensor_tensor(sc0, m1, m2, ALU.subtract)
            v.tensor_tensor(sc0, sc0, sc0, ALU.mult)
            v.tensor_tensor(sc1, v1, v2, ALU.add)
            v.tensor_scalar(sc1, sc1, 0.5, None, ALU.mult)
            v.scalar_tensor_tensor(sc1, sc0, 0.25, sc1, ALU.mult, ALU.add)
            v.tensor_scalar(sc1, sc1, EPS, None, ALU.add)
            return v.reciprocal(sc0, sc1)
        P.st("v", stats_math, wait=wait0 if wait0 else [])
        a0 = P.mark("v", "AUX")
        P.st("s", lambda s: s.activation(rs, sc0, AF.Sqrt), wait=[a0])
        a1 = P.mark("s", "AUX")

        def newton(v):
            v.tensor_tensor(sc0, rs, rs, ALU.mult)
            v.tensor_tensor(sc0, sc0, sc1, ALU.mult)
            v.tensor_scalar(sc0, sc0, -0.5, 1.5, ALU.mult, ALU.add)
            v.tensor_tensor(rs, rs, sc0, ALU.mult)
            v.tensor_tensor(nm, mu, rs, ALU.mult)
            return v.tensor_scalar(nm, nm, -1.0, None, ALU.mult)
        P.st("v", newton, wait=[a1])
        return P.mark("v", "AUX")

    def emit_ln_transpose(stack, src_tiles, rs, aux_ready, src3d=False, aux_tiles=None):
        pTR = psum(stack, "pTR", [128, 3, NC, 128], BF16)
        lnt_hist = []
        lnv_hist = []
        for t in range(NT):
            par = t % 3
            ap = src_tiles(t)
            xout = (sXFB[:, par, :].rearrange("p (c q) -> p c q", q=128)
                    if src3d else sXFB[:, par, :])
            if aux_tiles is not None:
                w = [aux_tiles[t]]
            else:
                w = list(aux_ready) if isinstance(aux_ready, list) else [aux_ready]
            if t >= 3:
                w.append(lnt_hist[t - 3])
            P.st("s", lambda s, xout=xout, ap=ap, t=t:
                 s.activation(xout, ap, AF.Identity,
                              bias=rs[:, t, 1:2], scale=rs[:, t, 0:1]),
                 wait=w)
            lns = P.mark("s", "LNS")
            for c in range(NC):
                wt = [lns]
                if t >= 3 and c == 0:
                    wt.append(lnv_hist[t - 3])
                P.st("t", lambda tt, par=par, c=c:
                     tt.transpose(pTR[:, par, c, :], sXFB[:, par, 128 * c:128 * (c + 1)], sIDB[:]),
                     wait=wt)
            lnt_hist.append(P.mark("t", "LNT"))
            P.st("v", lambda v, t=t, par=par:
                 v.tensor_copy(sT[:, 0:NC, 128 * t:128 * (t + 1)], pTR[:, par]),
                 wait=[lnt_hist[-1]])
            lnv_hist.append(P.mark("v", "LNV"))
        return lnv_hist

    def emit_ffn(first=False, pre_ag=None, post_stats=False, store_out=False):
        import os as _os
        _fcut = int(_os.environ.get("KERN_FFNCUT", "99"))
        fes = ExitStack()
        sW1C = sbuf(fes, "sW1C", [128, NC, 256], BF16)
        sW2C = sbuf(fes, "sW2C", [128, 2, 2], BF16)
        sW1F = sbuf(fes, "sW1F", [2, D], BF16)
        sW2BD = sbuf(fes, "sW2BD", [128, NC, 64], BF16)
        sDIR = sbuf(fes, "sDIR", [64, D], BF16)
        sDIRT = sbuf(fes, "sDIRT", [128, NC, 64], BF16)
        sCA = sbuf(fes, "sCA", [128, NT * 72], F32)
        sCB = sbuf(fes, "sCB", [128, NT * 72], F32)
        sVAL = sbuf(fes, "sVAL", [128, NT, 72], F32)
        sGT = sbuf(fes, "sGT", [128, NT, 72], F32)
        sCOH = sbuf(fes, "sCOH", [128, NT, 8], F32)
        sLOH = sbuf(fes, "sLOH", [128, NT, 8], F32)
        sSELG = sbuf(fes, "sSELG", [128, NT, 8], F32)
        sTMP8 = sbuf(fes, "sTMP8", [128, NT, 8], F32)
        sSEL = sbuf(fes, "sSEL", [128, NT, 4], F32)
        scTs = sbuf(fes, "scTs", [72, N], F32)
        sHT = sbuf(fes, "sHT", [128, 2, N], BF16)
        sCT = sbuf(fes, "sCT", [2, N], BF16)
        sHH = sbuf(fes, "sHH", [128, 2, N], BF16)
        sMAGT = sbuf(fes, "sMAGT", [64, N], BF16)
        sDOT = sbuf(fes, "sDOT", [64, N], BF16)
        sMGm = sbuf(fes, "sMGm", [128, NT, 64], BF16)
        sDTm = sbuf(fes, "sDTm", [128, NT, 64], BF16)
        sWSEL = sbuf(fes, "sWSEL", [128, NT, 64], BF16)
        sWT = sbuf(fes, "sWT", [64, N], BF16)
        sSC64 = sbuf(fes, "sSC64", [128, NT, 64], F32)
        sYS = sbuf(fes, "sYS", [128, NT, 4], F32)
        sDS2 = sbuf(fes, "sDS2", [64, 2], F32)
        sDR1 = sbuf(fes, "sDR1", [1, 64], F32)
        sDR2 = sbuf(fes, "sDR2", [1, 64], F32)
        sDIRG = sbuf(fes, "sDIRG", [64, D], BF16)
        sWS = sbuf(fes, "sWS", [128, 32], F32)
        sWSB = sbuf(fes, "sWSB", [128, 32], BF16)
        sRT = sbuf(fes, "sRT", [2, 16, 128], BF16)

        dma("g", sW1C[:], d_cw1.rearrange("(c p) n -> p c n", p=128), sem="LGW")
        dma("g", sW2C[:], d_cw2.rearrange("(c p) n -> p c n", p=128), sem="LGW")
        dma("g", sW1F[:], d_w1f, sem="LGW")
        dma("g", sW2BD[:], d_w2bd.rearrange("(c p) n -> p c n", p=128), sem="LGW")
        dma("g", sDIR[:], d_dirs, sem="LGD")
        ldg_d = ("LGD", P.val["LGD"])
        dma("g", sDIRT[:], d_dirsT.rearrange("(c p) s -> p c s", p=128), sem="LGW")
        ldg_w = ("LGW", P.val["LGW"])
        if first:
            tern_es = ExitStack()
            sSGR = sbuf(tern_es, "sSGR", [128, NC, 64], F32)
            sTMPC = sCA[:, 0:512].rearrange("p (c s) -> p c s", c=NC)
            dma("y", sSGR[:], d_sigsT.rearrange("(c p) s -> p c s", p=128), sem="LDSG")
            ld_sg = ("LDSG", P.val["LDSG"])

        if pre_ag is not None:
            aux = emit_ln_stats_tail(sRS, [pre_ag])
        elif first:
            tw = [[xld[t // 4]] for t in range(NT)]
            aux = emit_ln_stats(lambda t: sX[:, t, :], sRS, tile_waits=tw, quarters=True)
        else:
            aux = emit_ln_stats(lambda t: sX[:, t, :], sRS)

        if first:
            def tern(v):
                v.tensor_scalar(sTMPC[:], sSGR[:], 0.3, None, ALU.is_gt)
                v.tensor_scalar(sSGR[:], sSGR[:], -0.3, None, ALU.is_lt)
                v.tensor_tensor(sSG[:, :, 0:64], sTMPC[:], sSGR[:], ALU.subtract)
                v.reduce_sum(sTMPC[:, :, 0:8],
                             sSG[:, :, 0:64].rearrange("p c (l j) -> p c l j", j=8), axis=AX.X)
                v.tensor_scalar(sTMPC[:, :, 8:16], sTMPC[:, :, 0:8], 0.0, None, ALU.is_gt)
                v.tensor_scalar(sTMPC[:, :, 16:24], sTMPC[:, :, 0:8], 0.0, None, ALU.is_lt)
                return v.tensor_tensor(sSG[:, :, 64:72], sTMPC[:, :, 8:16], sTMPC[:, :, 16:24],
                                       ALU.subtract)
            P.st("v", tern, wait=[ld_sg])
            P.mark("v", "AUX")
            tern_es.close()
        if _fcut <= 1:
            P.barrier()
            fes.close()
            return

        lnes = ExitStack()
        if isinstance(aux, list):
            lnv = emit_ln_transpose(lnes, lambda t: sX[:, t, :], sRS, None,
                                    aux_tiles=[aux[t // 4] for t in range(NT)])
        else:
            lnv = emit_ln_transpose(lnes, lambda t: sX[:, t, :], sRS, aux)

        ses = ExitStack()
        psc = psum(ses, "psc", [72, N], F32)

        if first:
            dd_es = ExitStack()
            sDSQ = sbuf(dd_es, "sDSQ", [64, D], BF16)
            pDR = psum(dd_es, "pDR", [128, 128], F32)
            P.st("s", lambda s: s.activation(sDSQ[:], sDIR[:], AF.Square),
                 wait=[ldg_d])
            a_dsq = P.mark("s", "AUX")

            def dsum(v):
                v.reduce_sum(sDS2[:, 0:1], sDIR[:].rearrange("p (o f) -> p o f", o=1), axis=AX.X)
                return v.reduce_sum(sDS2[:, 1:2], sDSQ[:].rearrange("p (o f) -> p o f", o=1),
                                    axis=AX.X)
            P.st("v", dsum, wait=[a_dsq, ldg_d])
            a_ds = P.mark("v", "AUX")
            def dtr(t):
                t.transpose(pDR[0:1, 0:64], sDS2[:, 0:1], sIDF[0:64, 0:64])
                return t.transpose(pDR[0:1, 64:128], sDS2[:, 1:2], sIDF[0:64, 0:64])
            P.st("t", dtr, wait=[a_ds])
            a_dt = P.mark("t", "AUX")

            def drows(v):
                v.tensor_copy(sDR1[:], pDR[0:1, 0:64])
                return v.tensor_copy(sDR2[:], pDR[0:1, 64:128])
            P.st("v", drows, wait=[a_dt])
            a_dv = P.mark("v", "AUX")
            P.st("t", lambda t: t.matmul(pDR[:, 0:64], sONESF[:], sDR1[:], start=True, stop=True),
                 wait=[a_dv])
            a_b0 = P.mark("t", "AUX")
            P.st("v", lambda v: v.tensor_copy(sDSB[:, 0, :], pDR[:, 0:64]), wait=[a_b0])
            a_b1 = P.mark("v", "AUX")
            P.st("t", lambda t: t.matmul(pDR[:, 0:64], sONESF[:], sDR2[:], start=True, stop=True),
                 wait=[a_b1])
            a_b2 = P.mark("t", "AUX")
            P.st("v", lambda v: v.tensor_copy(sDSB[:, 1, :], pDR[:, 0:64]), wait=[a_b2])
            P.mark("v", "AUX")
            dd_es.close()
        P.st("v", lambda v: v.tensor_tensor(sDIRG[:], sDIR[:], sGB[0:64, 0, :], ALU.mult),
             wait=[ldg_d])
        P.mark("v", "AUX")
        if _fcut <= 3:
            P.barrier()
            fes.close()
            return

        for s4 in range(4):
            def f(t, s4=s4):
                mi = None
                for c in range(NC):
                    mi = t.matmul(psc[:, 512 * s4:512 * (s4 + 1)],
                                  sSG[:, c, :], sT[:, c, 512 * s4:512 * (s4 + 1)],
                                  start=(c == 0), stop=(c == NC - 1))
                return mi
            P.st("t", f, wait=[lnv[4 * s4 + 3]])
        a_sc = P.mark("t", "AUX")
        P.st("v", lambda v: v.tensor_copy(scTs[:], psc[:]), wait=[a_sc])
        sc_evc = P.mark("v", "AUX")
        ses.close()
        lnes.close()
        if _fcut <= 4:
            P.barrier()
            fes.close()
            return

        des = ExitStack()
        pdt = psum(des, "pdt", [64, N], F32)
        for s4 in range(4):
            def f(t, s4=s4):
                mi = None
                for c in range(NC):
                    mi = t.matmul(pdt[:, 512 * s4:512 * (s4 + 1)],
                                  sDIRT[:, c, :], sT[:, c, 512 * s4:512 * (s4 + 1)],
                                  start=(c == 0), stop=(c == NC - 1))
                return mi
            P.st("t", f, wait=[sc_evc, ldg_w] if s4 == 0 else [])
        a_dt2 = P.mark("t", "AUX")
        P.st("s", lambda s: s.activation(sDOT[:], pdt[:], AF.Copy), wait=[a_dt2])
        dot_evc = P.mark("s", "AUX")
        des.close()
        if _fcut <= 5:
            P.barrier()
            fes.close()
            return

        tes = ExitStack()
        pTF = psum(tes, "pTF", [128, 2, 4, 128], F32)
        evc_hist = []
        for g in range(4):
            par = g & 1
            for k in range(4):
                t = 4 * g + k
                w = []
                if g == 0 and k == 0:
                    w.append(dot_evc)
                if g >= 2 and k == 0:
                    w.append(evc_hist[g - 2])
                P.st("t", lambda tt, t=t, par=par, k=k:
                     tt.transpose(pTF[:, par, k, 0:72], scTs[:, 128 * t:128 * (t + 1)],
                                  sIDF[0:72, 0:72]), wait=w)
            trid = P.mark("t", "TRID")
            P.st("v", lambda v, g=g, par=par:
                 v.tensor_copy(sGT[:, 4 * g:4 * (g + 1), :], pTF[:, par, :, 0:72]),
                 wait=[trid])
            evc_hist.append(P.mark("v", "EVC"))
        tes.close()
        if _fcut <= 6:
            P.barrier()
            fes.close()
            return

        gtf = sGT[:].rearrange("p t s -> p (t s)")
        valf = sVAL[:].rearrange("p t s -> p (t s)")
        P.st("v", lambda v: v.tensor_scalar(sCB[:], gtf, sKN[:, 16:17], 87.0, ALU.mult, ALU.min),
             wait=[evc_hist[-1]])
        pre_exp = P.mark("v", "AUX")
        P.st("s", lambda s: s.activation(sCA[:], sCB[:], AF.Exp), wait=[pre_exp])
        a_exp = P.mark("s", "AUX")

        def calib(v):
            v.tensor_scalar(sCB[:], sCA[:], 1.0, None, ALU.add)
            v.reciprocal(sCA[:], sCB[:])
            v.tensor_scalar(sCA[:], sCA[:], 7.0, None, ALU.mult)
            v.tensor_scalar(sCB[:], sCA[:], 1.0, None, ALU.min)
            v.tensor_scalar(valf, sCB[:], sKN[:, 8:9], sKN[:, 0:1], ALU.mult, ALU.add)
            for j in range(1, 7):
                v.tensor_scalar(sCB[:], sCA[:], float(-j), 0.0, ALU.add, ALU.max)
                v.tensor_scalar(sCB[:], sCB[:], 1.0, None, ALU.min)
                v.scalar_tensor_tensor(valf, sCB[:], sKN[:, 8 + j:9 + j], valf, ALU.mult, ALU.add)
            return v.tensor_copy(sSEL[:, :, 0:1], sVAL[:, :, 0:1])
        P.st("v", calib, wait=[a_exp])
        cal_aux = P.mark("v", "AUX")
        if _fcut <= 7:
            P.barrier()
            fes.close()
            return

        def argmax_block(v, src, oh, col):
            v.reduce_max(sSEL[:, :, col], src, axis=AX.X)
            v.tensor_tensor(oh, src, bcast0(sSEL[:, :, col:col + 1], 2, 8), ALU.is_equal)
            v.tensor_tensor(oh, oh,
                            bcast0(sPRIO[:].rearrange("p (o s) -> p o s", o=1), 1, NT), ALU.mult)
            v.reduce_max(sSEL[:, :, col + 1], oh, axis=AX.X)
            return v.tensor_tensor(oh, oh, bcast0(sSEL[:, :, col + 1:col + 2], 2, 8), ALU.is_equal)

        P.st("v", lambda v: argmax_block(v, sVAL[:, :, 64:72], sCOH[:], 0), wait=[cal_aux])
        sel1_aux = P.mark("v", "AUX")

        def select2(v):
            for c in range(8):
                v.tensor_tensor(sTMP8[:], sVAL[:, :, 8 * c:8 * (c + 1)],
                                bcast0(sCOH[:, :, c:c + 1], 2, 8), ALU.mult)
                if c == 0:
                    v.tensor_copy(sSELG[:], sTMP8[:])
                else:
                    v.tensor_tensor(sSELG[:], sSELG[:], sTMP8[:], ALU.add)
            return argmax_block(v, sSELG[:], sLOH[:], 2)
        P.st("v", select2, wait=[sel1_aux])
        sel_aux = P.mark("v", "AUX")
        if _fcut <= 8:
            P.barrier()
            fes.close()
            return

        ces = ExitStack()
        pH = psum(ces, "pH", [128, N], F32)
        for half in range(2):
            for c in range(NC):
                def f(t, c=c, half=half):
                    mi = None
                    for s4 in range(4):
                        mi = t.matmul(pH[:, 512 * s4:512 * (s4 + 1)],
                                      sW1C[:, c, 128 * half:128 * (half + 1)],
                                      sT[:, c, 512 * s4:512 * (s4 + 1)],
                                      start=(c == 0), stop=(c == NC - 1))
                    return mi
                w = []
                if half == 0 and c == 0:
                    w = [evc_hist[-1], ldg_w]
                if half == 1 and c == 0:
                    w = [P.cur("EVC_s")]
                P.st("t", f, wait=w)
            a_h = P.mark("t", "AUX")
            P.st("s", lambda s, half=half:
                 s.activation(sHT[:, half, :], pH[:], AF.Gelu, bias=sBIAS[:, 12 + half:13 + half]),
                 wait=[a_h])
            P.mark("s", "EVC")
        ces.close()
        if _fcut <= 9:
            P.barrier()
            fes.close()
            return

        c2es = ExitStack()
        pC = psum(c2es, "pC", [2, N], F32)
        for c2 in range(2):
            def f(t, c2=c2):
                mi = None
                for s4 in range(4):
                    mi = t.matmul(pC[:, 512 * s4:512 * (s4 + 1)],
                                  sW2C[:, c2, :], sHT[:, c2, 512 * s4:512 * (s4 + 1)],
                                  start=(c2 == 0), stop=(c2 == 1))
                return mi
            P.st("t", f, wait=[P.cur("EVC_s"), ldg_w] if c2 == 0 else [])
        a_c = P.mark("t", "AUX")
        P.st("s", lambda s: s.activation(sCT[:], pC[:], AF.Tanh, bias=sCB2[:]),
             wait=[a_c])
        tanh_evc = P.mark("s", "EVC")
        c2es.close()
        if _fcut <= 10:
            P.barrier()
            fes.close()
            return

        s2es = ExitStack()
        pS1 = psum(s2es, "pS1", [128, N], F32)
        pM = psum(s2es, "pM", [64, N], F32)
        mag_done = []
        relu_done = []
        for oc in range(NC):
            par = oc & 1
            def fhh(t, oc=oc, par=par):
                mi = None
                for s4 in range(4):
                    mi = t.matmul(pS1[:, 512 * s4:512 * (s4 + 1)],
                                  sW1F[:, 128 * oc:128 * (oc + 1)],
                                  sCT[:, 512 * s4:512 * (s4 + 1)],
                                  start=True, stop=True)
                return mi
            w = []
            if oc == 0:
                w.append(tanh_evc)
                w.append(ldg_w)
            if oc >= 1:
                w.append(relu_done[oc - 1])
            P.st("t", fhh, wait=w)
            a_hh = P.mark("t", "AUX")
            P.st("s", lambda s, oc=oc, par=par:
                 s.activation(sHH[:, par, :], pS1[:], AF.Relu, bias=sBIAS[:, 14 + oc:15 + oc]),
                 wait=[a_hh])
            hh_evc = P.mark("s", "EVC")
            relu_done.append(hh_evc)
            def fmag(t, oc=oc, par=par):
                mi = None
                for s4 in range(4):
                    mi = t.matmul(pM[:, 512 * s4:512 * (s4 + 1)],
                                  sW2BD[:, oc, :], sHH[:, par, 512 * s4:512 * (s4 + 1)],
                                  start=(oc == 0), stop=(oc == NC - 1))
                return mi
            P.st("t", fmag, wait=[hh_evc])
            mag_done.append(P.mark("t", "AUX3"))
        P.st("s", lambda s: s.activation(sMAGT[:], pM[:], AF.Identity, bias=sB2SP[:]),
             wait=[mag_done[-1]])
        mag_evc = P.mark("s", "EVC")
        s2es.close()
        if _fcut <= 11:
            P.barrier()
            fes.close()
            return

        t2es = ExitStack()
        pTB = psum(t2es, "pTB", [128, 2, 8, 128], BF16)
        tb_evc = []
        gi = 0
        for src, dst in ((sMAGT, sMGm), (sDOT, sDTm)):
            for g in range(2):
                par = gi & 1
                for k in range(8):
                    t = 8 * g + k
                    w = [tb_evc[gi - 2]] if (gi >= 2 and k == 0) else []
                    if gi == 0 and k == 0:
                        w.append(mag_evc)
                    P.st("t", lambda tt, t=t, par=par, k=k, src=src:
                         tt.transpose(pTB[:, par, k, 0:64], src[:, 128 * t:128 * (t + 1)],
                                      sIDB[0:64, 0:64]), wait=w)
                trid = P.mark("t", "TRID")
                P.st("v", lambda v, g=g, par=par, dst=dst:
                     v.tensor_copy(dst[:, 8 * g:8 * (g + 1), :], pTB[:, par, :, 0:64]),
                     wait=[trid])
                tb_evc.append(P.mark("v", "EVC"))
                gi += 1
        t2es.close()
        if _fcut <= 12:
            P.barrier()
            fes.close()
            return

        def wsel(v):
            for c in range(8):
                v.tensor_tensor(sTMP8[:], sLOH[:], bcast0(sCOH[:, :, c:c + 1], 2, 8), ALU.mult)
                v.tensor_tensor(sWSEL[:, :, 8 * c:8 * (c + 1)], sTMP8[:],
                                sMGm[:, :, 8 * c:8 * (c + 1)], ALU.mult)
            wf = sWSEL[:].rearrange("p t s -> p (t s)")
            return v.tensor_scalar(wf, wf, sKN[:, 17:18], None, ALU.mult)
        P.st("v", wsel, wait=[sel_aux, tb_evc[-1]])
        wsel_aux = P.mark("v", "AUX")
        if _fcut <= 13:
            P.barrier()
            fes.close()
            return

        nm2rs = sYS[:, :, 0:1]
        mu_y = sYS[:, :, 0:1].rearrange("p t o -> p (t o)")
        cross = sYS[:, :, 1:2].rearrange("p t o -> p (t o)")
        ffnsq = sYS[:, :, 2:3].rearrange("p t o -> p (t o)")
        exf2 = sYS[:, :, 3:4].rearrange("p t o -> p (t o)")
        rs2 = sRS2[:, :, 0:1].rearrange("p t o -> p (t o)")
        nm2 = sRS2[:, :, 1:2].rearrange("p t o -> p (t o)")
        rs1v = sRS[:, :, 0:1].rearrange("p t o -> p (t o)")
        nm1v = sRS[:, :, 1:2].rearrange("p t o -> p (t o)")
        sc0 = sSCR[:, :, 0:1].rearrange("p t o -> p (t o)")
        sc1b = sSCR[:, :, 1:2].rearrange("p t o -> p (t o)")

        def ystats(v):
            v.tensor_tensor(sSC64[:], sWSEL[:], bcast0(sDSB[:, 0:1, :], 1, NT), ALU.mult)
            v.reduce_sum(sYS[:, :, 0], sSC64[:], axis=AX.X)
            v.tensor_tensor(sSC64[:], sWSEL[:], sDTm[:], ALU.mult)
            v.reduce_sum(sYS[:, :, 1], sSC64[:], axis=AX.X)
            v.tensor_tensor(sSC64[:], sWSEL[:], sWSEL[:], ALU.mult)
            v.tensor_tensor(sSC64[:], sSC64[:], bcast0(sDSB[:, 1:2, :], 1, NT), ALU.mult)
            v.reduce_sum(sYS[:, :, 2], sSC64[:], axis=AX.X)
            v.tensor_tensor(exf2, rs1v, rs1v, ALU.mult)
            v.tensor_scalar(exf2, exf2, -EPS, 1.0, ALU.mult, ALU.add)
            v.tensor_scalar(mu_y, mu_y, 1.0 / 1024.0, None, ALU.mult)
            v.tensor_scalar(sc0, cross, 2.0 / 1024.0, None, ALU.mult)
            v.scalar_tensor_tensor(sc0, ffnsq, 1.0 / 1024.0, sc0, ALU.mult, ALU.add)
            v.tensor_tensor(sc0, sc0, exf2, ALU.add)
            v.tensor_tensor(sc1b, mu_y, mu_y, ALU.mult)
            v.tensor_tensor(sc1b, sc0, sc1b, ALU.subtract)
            v.tensor_scalar(sc1b, sc1b, EPS, None, ALU.add)
            return v.reciprocal(sc0, sc1b)
        P.st("v", ystats, wait=[wsel_aux])
        a_ys = P.mark("v", "AUX")
        ws_mu = sWS[:].rearrange("p (t k) -> p k t", k=2)[:, 0, :]
        ws_sig = sWS[:].rearrange("p (t k) -> p k t", k=2)[:, 1, :]

        def sig_s(s):
            s.activation(rs2, sc0, AF.Sqrt)
            return s.activation(ws_sig, sc1b, AF.Sqrt)
        P.st("s", sig_s, wait=[a_ys])
        a_ys2 = P.mark("s", "AUX")

        def newton2(v):
            v.tensor_tensor(sc0, rs2, rs2, ALU.mult)
            v.tensor_tensor(sc0, sc0, sc1b, ALU.mult)
            v.tensor_scalar(sc0, sc0, -0.5, 1.5, ALU.mult, ALU.add)
            v.tensor_tensor(rs2, rs2, sc0, ALU.mult)
            v.tensor_scalar(ws_mu, mu_y, -1.0, None, ALU.mult)
            return v.tensor_copy(sWSB[:], sWS[:])
        P.st("v", newton2, wait=[a_ys2])
        P.mark("v", "AUX")
        P.barrier()
        if _fcut <= 14:
            fes.close()
            return

        t3es = ExitStack()
        pTW = psum(t3es, "pTW", [64, 2, 8, 128], BF16)
        pRT = psum(t3es, "pRT", [2, 16, 128], BF16)
        for g in range(2):
            par = g & 1
            for k in range(8):
                t = 8 * g + k
                P.st("t", lambda tt, t=t, par=par, k=k:
                     tt.transpose(pTW[:, par, k, :], sWSEL[:, t, :], sIDB[:]))
            trid = P.mark("t", "TRID")
            P.st("v", lambda v, g=g, par=par:
                 v.tensor_copy(sWT[:, 1024 * g:1024 * (g + 1)].rearrange("p (k q) -> p k q", k=8),
                               pTW[:, par]),
                 wait=[trid])
            P.mark("v", "EVC")
        for t in range(NT):
            P.st("t", lambda tt, t=t:
                 tt.transpose(pRT[:, t, :], sWSB[:, 2 * t:2 * t + 2], sIDB[:]))
        rt_tr = P.mark("t", "TRID")
        P.st("v", lambda v: v.tensor_copy(sRT[:], pRT[:]), wait=[rt_tr])
        P.mark("v", "EVC")
        P.barrier()
        t3es.close()
        if _fcut <= 15:
            P.barrier()
            fes.close()
            return

        d2es = ExitStack()
        pD = psum(d2es, "pD", [128, 2, D], F32)
        ztail_hist = []
        for t in range(NT):
            par = t & 1
            def fdir(tt, t=t, par=par):
                for s2 in range(2):
                    tt.matmul(pD[:, par, 512 * s2:512 * (s2 + 1)],
                              sWT[:, 128 * t:128 * (t + 1)],
                              sDIRG[:, 512 * s2:512 * (s2 + 1)],
                              start=True, stop=False)
                for s2 in range(2):
                    tt.matmul(pD[:, par, 512 * s2:512 * (s2 + 1)],
                              sRT[:, t, :],
                              sGBr[:, 512 * s2:512 * (s2 + 1)],
                              start=False, stop=False)
                mi = None
                for c in range(NC):
                    mi = tt.matmul(pD[:, par, 128 * c:128 * (c + 1)],
                                   sT[:, c, 128 * t:128 * (t + 1)],
                                   sDG[:, c, :],
                                   start=False, stop=(c % 4 == 3))
                return mi
            w = [ldgb] if t == 0 else []
            if t >= 2:
                w.append(ztail_hist[t - 2])
            P.st("t", fdir, wait=w)
            a_t = P.mark("t", "AUX")

            P.st("v", lambda v, t=t, par=par:
                 v.scalar_tensor_tensor(sX[:, t, :], pD[:, par, :], sRS2[:, t, 0:1],
                                        sX[:, t, :], ALU.mult, ALU.add),
                 wait=[a_t])
            ztail_hist.append(P.mark("v", "EVC"))
            if post_stats:
                P.st("v", lambda v, t=t: v.bn_stats(sSTAT[:, t, 0:6], sX[:, t, 0:512]),
                     wait=[ztail_hist[-1]])
                P.st("v", lambda v, t=t: v.bn_stats(sSTAT[:, t, 6:12], sX[:, t, 512:1024]))
            if store_out:
                P.st("y", lambda e, t=t: e.dma_start(
                    out=dy.rearrange("(i p) f -> p i f", p=128)[:, t, :],
                    in_=sX[:, t, :]),
                    wait=[ztail_hist[-1]], inc=[("ST", 16)])
        if post_stats:
            bn_mark = P.mark("v", "AUX")
            for t in range(NT):
                P.st("v", lambda v, t=t: v.bn_aggr(sMV4[:, t, 0:2], sSTAT[:, t, 0:6]),
                     wait=[bn_mark] if t == 0 else [])
                P.st("v", lambda v, t=t: v.bn_aggr(sMV4[:, t, 2:4], sSTAT[:, t, 6:12]))
            post_ag[0] = P.mark("v", "AUX")
        P.barrier()
        d2es.close()
        fes.close()

    post_ag = [None]

    def emit_attn():
        import os as _os
        _acut = int(_os.environ.get("KERN_ATTCUT", "99"))
        if post_ag[0] is not None:
            aux = emit_ln_stats_tail(sRS, [post_ag[0]])
        else:
            aux = emit_ln_stats(lambda t: sX[:, t, :], sRS)
        ln2es = ExitStack()
        lnv2 = emit_ln_transpose(ln2es, lambda t: sX[:, t, :], sRS, aux)
        ln2es.close()

        if _acut <= 1:
            return
        aes = ExitStack()
        sNB = sbuf(aes, "sNB", [128, 8], F32)
        sMX = sbuf(aes, "sMX", [128, 12], F32)

        qk_es = ExitStack()
        sQT = sbuf(qk_es, "sQT", [128, NC, N], BF16)
        sKT = sbuf(qk_es, "sKT", [128, 2, N], BF16)
        sVTM = sbuf(qk_es, "sVTM", [128, NT, 4, 65], BF16)

        qkv_es = ExitStack()
        sVT = sbuf(qkv_es, "sVT", [64, 4, N], BF16)
        sWQ = sbuf(qkv_es, "sWQ", [128, NC, D], BF16)
        sWK = sbuf(qkv_es, "sWK", [128, NC, 256], BF16)
        sWV = sbuf(qkv_es, "sWV", [128, NC, 256], BF16)
        dma("g", sWQ[:], d_wq.rearrange("(c p) n -> p c n", p=128), sem="LGQ")
        dma("g", sWK[:], d_wk.rearrange("(c p) n -> p c n", p=128), sem="LGQ")
        dma("g", sWV[:], d_wv.rearrange("(c p) n -> p c n", p=128), sem="LGQ")
        ldg_qkv = ("LGQ", P.val["LGQ"])

        pQ_es = ExitStack()
        pQ = psum(pQ_es, "pQ", [128, 2, N], F32)
        jobs = ([("q", oc) for oc in range(NC)] +
                [("k", 0), ("k", 1), ("v", 0), ("v", 1)])
        wtab = {"q": sWQ, "k": sWK, "v": sWV}
        evac_hist = []
        for ji, (kind, oc) in enumerate(jobs):
            par = ji & 1
            def fproj(t, kind=kind, oc=oc, par=par):
                wt = wtab[kind]
                mi = None
                for s4 in range(4):
                    for c in range(NC):
                        mi = t.matmul(pQ[:, par, 512 * s4:512 * (s4 + 1)],
                                      wt[:, c, 128 * oc:128 * (oc + 1)],
                                      sT[:, c, 512 * s4:512 * (s4 + 1)],
                                      start=(c == 0), stop=(c == NC - 1))
                return mi
            w = [ldg_qkv, lnv2[NT - 1]] if ji == 0 else []
            if ji >= 2:
                w.append(evac_hist[ji - 2])
            P.st("t", fproj, wait=w)
            a_j = P.mark("t", "AUX")
            if kind == "v":
                def vevac(s, oc=oc, par=par):
                    s.activation(sVT[0:64, 2 * oc, :], pQ[0:64, par, :], AF.Identity,
                                 bias=sBIAS[0:64, 10 + oc:11 + oc], scale=1.0)
                    return s.activation(sVT[0:64, 2 * oc + 1, :], pQ[64:128, par, :], AF.Identity,
                                        bias=sBIAS[64:128, 10 + oc:11 + oc], scale=1.0)
                P.st("s", vevac, wait=[a_j])
            else:
                if kind == "q":
                    dst, bias, scale = sQT[:, oc, :], sBIAS[:, oc:oc + 1], 0.125
                else:
                    dst, bias, scale = sKT[:, oc, :], sBIAS[:, 8 + oc:9 + oc], 1.0
                P.st("s", lambda s, dst=dst, bias=bias, scale=scale, par=par:
                     s.activation(dst, pQ[:, par, :], AF.Identity, bias=bias, scale=scale),
                     wait=[a_j])
            evac_hist.append(P.mark("s", "EVC"))
        P.barrier()
        pQ_es.close()
        if _acut <= 2:
            qkv_es.close()
            qk_es.close()
            aes.close()
            return

        vt_es = ExitStack()
        pTB2 = psum(vt_es, "pTB2", [128, 2, 8, 128], BF16)
        P.st("v", lambda v: v.memset(sVTM[:], 1.0))
        vtm_ms = P.mark("v", "AUX")
        vtm_evc = []
        for mg in range(8):
            par = mg & 1
            for k in range(8):
                mloc = mg * 2 + k // 4
                g = k % 4
                w = [vtm_evc[mg - 2]] if (mg >= 2 and k == 0) else []
                P.st("t", lambda tt, mloc=mloc, g=g, par=par, k=k:
                     tt.transpose(pTB2[:, par, k, 0:64],
                                  sVT[0:64, g, 128 * mloc:128 * (mloc + 1)],
                                  sIDB[0:64, 0:64]), wait=w)
            trid = P.mark("t", "TRID")
            P.st("v", lambda v, mg=mg, par=par:
                 v.tensor_copy(sVTM[:, 2 * mg:2 * mg + 2, :, 0:64],
                               pTB2[:, par, :, 0:64].rearrange("p (m g) d -> p m g d", g=4)),
                 wait=[trid] + ([vtm_ms] if mg == 0 else []))
            vtm_evc.append(P.mark("v", "EVC"))
        P.barrier()
        vt_es.close()
        qkv_es.close()
        if _acut <= 3:
            qk_es.close()
            aes.close()
            return
        sATM = sbuf(aes, "sATM", [128, NT, 16, 65], BF16, side="right")

        bb_es = ExitStack()
        sSQB = sbuf(bb_es, "sSQB", [128, 2, N], BF16)
        pBB = psum(bb_es, "pBB", [128, 2, N], F32)
        red_hist = []
        for ci in range(10):
            par = ci & 1
            src = sQT[:, ci, :] if ci < 8 else sKT[:, ci - 8, :]
            w = []
            if ci >= 2:
                w.append(red_hist[ci - 2])
            P.st("s", lambda s, src=src, par=par: s.activation(sSQB[:, par, :], src, AF.Square),
                 wait=w)
            a_sq = P.mark("s", "AUX")
            def fbb(t, par=par):
                mi = None
                for s4 in range(4):
                    mi = t.matmul(pBB[:, par, 512 * s4:512 * (s4 + 1)], sONESQB[:],
                                  sSQB[:, par, 512 * s4:512 * (s4 + 1)],
                                  start=True, stop=True)
                return mi
            P.st("t", fbb, wait=[a_sq])
            a_bb = P.mark("t", "AUX")
            P.st("v", lambda v, ci=ci, par=par:
                 v.reduce_max(sMX[:, ci:ci + 1],
                              pBB[:, par, :].rearrange("p (o f) -> p o f", o=1), axis=AX.X),
                 wait=[a_bb])
            red_hist.append(P.mark("v", "AUX3"))

        P.st("v", lambda v: v.tensor_tensor(
                 sNB[:].rearrange("p (a b) -> p a b", b=2),
                 sMX[:, 0:8].rearrange("p (a b) -> p a b", b=2),
                 bcast0(sMX[:, 8:10].rearrange("p (o b) -> p o b", o=1), 1, 4), ALU.mult),
             wait=[red_hist[-1]])
        a_bc = P.mark("v", "AUX")
        P.st("s", lambda s: s.activation(sNB[:], sNB[:], AF.Sqrt, scale=1.1025),
             wait=[a_bc])
        a_bs = P.mark("s", "AUX")
        P.st("v", lambda v: v.tensor_scalar(sNB[:], sNB[:], -1.0, None, ALU.mult),
             wait=[a_bs])
        P.mark("v", "AUX")
        P.barrier()
        bb_es.close()
        if _acut <= 4:
            sc_dummy = None
            qk_es.close()
            aes.close()
            return

        sc_es = ExitStack()
        pS = psum(sc_es, "pS", [128, 2, 2, 512], F32)
        pO = psum(sc_es, "pO", [65, 2, 512], F32)
        ptrA = psum(sc_es, "ptrA", [128, 8, 66], BF16)
        sE = sbuf(sc_es, "sE", [128, 4, 2, 512], BF16)
        sTMPA = sbuf(sc_es, "sTMPA", [65, 2, 512], BF16)
        sRC = sbuf(sc_es, "sRC", [128, 4, 16], F32)
        sANBv = sT[:].rearrange("p c (t q) -> p t c q", t=NT)

        prev_atm_evc = 0
        prev_po_evc = 0
        prev_trid_tmpa = 0
        pending_evac = []
        pending_bnd = []
        since_evac = [99]
        prev_sq = [0]
        prev_t1 = [0]
        prev_rr = [0]
        pending_sqj = []

        def emit_sq_flush():
            while pending_sqj:
                ts4, t1m = pending_sqj.pop(0)
                for i, t in enumerate(ts4):
                    w = []
                    if i == 0:
                        w = [t1m]
                        if prev_sq[0]:
                            w.append(prev_sq[0])
                    if i == 3:
                        w = [P.mark("s", "SQ")]
                    P.st("s", lambda s, t=t, slot=i % 3:
                         s.activation(sXFB[:, slot, :].rearrange("p (c q) -> p c q", q=128),
                                      sANBv[:, t], AF.Square,
                                      accum_out=sMV4[:, t, 1:2]),
                         wait=w)
                prev_sq[0] = P.mark("s", "SQ")

        def emit_norm_j(J, am):
            ts4 = list(range(4 * J, 4 * (J + 1)))
            for i, t in enumerate(ts4):
                w = [am] if i == 0 else []
                if i == 0 and prev_t1[0]:
                    w.append(prev_t1[0])
                P.st("v", lambda v, t=t:
                     v.reciprocal(sRC[:, t % 4, :],
                                  sATM[:, t, :, 64:65].rearrange("p h o -> p (h o)")),
                     wait=w)
            rcm = P.mark("v", "AUX3")
            emit_sq_flush()
            for i, t in enumerate(ts4):
                P.st("v", lambda v, t=t:
                     v.tensor_tensor(
                         sANBv[:, t].rearrange("p c (hh d) -> p c hh d", hh=2),
                         sATM[:, t, :, 0:64].rearrange("p (c hh) d -> p c hh d", hh=2),
                         bcast0(sRC[:, t % 4, :].rearrange("p (c hh o) -> p c hh o",
                                                           hh=2, o=1), 3, 64),
                         ALU.mult),
                     wait=[rcm] if i == 0 else [])
            prev_t1[0] = P.mark("v", "AUX3")
            for i, t in enumerate(ts4):
                P.st("v", lambda v, t=t:
                     v.reduce_sum(sSTAT[:, t, 0:8], sANBv[:, t], axis=AX.X),
                     wait=[prev_t1[0]] if i == 0 else [])
            r1m = P.mark("v", "AUX3")
            for i, t in enumerate(ts4):
                P.st("v", lambda v, t=t:
                     v.reduce_sum(sMV4[:, t, 0:1],
                                  sSTAT[:, t, 0:8].rearrange("p (o c) -> p o c", o=1),
                                  axis=AX.X),
                     wait=[r1m] if i == 0 else [])
            prev_rr[0] = P.mark("v", "AUX3")
            pending_sqj.append((ts4, prev_t1[0]))
        exp_hist = {}
        pv_hist = {}
        gidx = 0
        lv_prev = [0]
        pend = []

        def emit_pv(args):
            g, hp, c, epar, lv, cmax, first = args
            wpv = [exp_hist[g]]
            if first and prev_po_evc:
                wpv.append(prev_po_evc)
            assert not (first and pending_evac), "first-of-J PV emitted before prior evac"
            def fpv(t, hp=hp, c=c, epar=epar, lv=lv, cmax=cmax):
                mi = None
                for h in range(2):
                    gg = (2 * hp + h) % 4
                    mi = t.matmul(pO[:, h, lv:512],
                                  sVTM[:, c, gg, :],
                                  sE[:, epar, h, lv:512],
                                  start=(c == 0), stop=(c == cmax - 1))
                return mi
            P.st("t", fpv, wait=wpv)
            pv_hist[g] = P.mark("t", "PVD")

        def emit_evac(last_g, bhp, bJ):
            nonlocal prev_po_evc
            for h in range(2):
                w = [pv_hist[last_g]]
                if h == 0 and prev_trid_tmpa:
                    w.append(prev_trid_tmpa)
                P.st("v", lambda v, h=h: v.tensor_copy(sTMPA[:, h, :], pO[:, h, :]),
                     wait=w)
            prev_po_evc = P.mark("v", "AUX")

            def boundary(hp=bhp, J=bJ, po_evc=prev_po_evc, atm_evc=prev_atm_evc):
                for tb in range(4):
                    for h in range(2):
                        w = [po_evc]
                        if tb == 0 and h == 0 and atm_evc:
                            w.append(atm_evc)
                        P.st("t", lambda tt, tb=tb, h=h:
                             tt.transpose(ptrA[:, tb * 2 + h, 0:65],
                                          sTMPA[:, h, 128 * tb:128 * (tb + 1)],
                                          sIDB[0:65, 0:65]), wait=w)
                trid = P.mark("t", "TRID")
                P.st("v", lambda v, hp=hp, J=J:
                     v.tensor_copy(sATM[:, 4 * J:4 * (J + 1), 2 * hp:2 * hp + 2, :],
                                   ptrA[:, :, 0:65].rearrange("p (tb h) d -> p tb h d", h=2)),
                     wait=[trid])
                am = P.mark("v", "AUX3")
                if hp == 7:
                    emit_norm_j(J, am)
                return trid, am
            pending_bnd.append(boundary)
            since_evac[0] = 0

        for J in range(4):
            cmax = 4 * J + 4
            for hp in range(8):
                kchunk = hp & 1
                for c in range(cmax):
                    g = gidx
                    gidx += 1
                    cpar = g & 1
                    epar = g & 3
                    lv = max(0, 128 * c - 512 * J)
                    q0 = 512 * J + lv
                    qe = 512 * J + 512
                    diag = (c >= 4 * J)
                    wqk = [exp_hist[g - 2]] if g >= 2 else []
                    def fqk(t, hp=hp, kchunk=kchunk, c=c, cpar=cpar, lv=lv, q0=q0, qe=qe):
                        mi = None
                        for h in range(2):
                            o = 64 * h
                            mi = t.matmul(pS[:, cpar, h, lv:512],
                                          sKT[o:o + 64, kchunk, 128 * c:128 * (c + 1)],
                                          sQT[o:o + 64, hp, q0:qe], start=True, stop=True,
                                          tile_position=(64 * h, 0))
                        return mi
                    P.st("t", fqk, wait=wqk)
                    qkd = P.mark("t", "QKD")
                    pend.append((g, hp, c, epar, lv, cmax, c == 0))
                    if len(pend) > 2:
                        emit_pv(pend.pop(0))
                    since_evac[0] += 1
                    if pending_bnd and since_evac[0] >= 2:
                        prev_trid_tmpa, prev_atm_evc = pending_bnd.pop(0)()
                    elif pending_evac and not pending_bnd and pending_evac[0][0] in pv_hist:
                        emit_evac(*pending_evac.pop(0))
                    wexp = [qkd]
                    if g >= 4:
                        wexp.append(pv_hist[g - 4])
                    P.st("s", lambda s, cpar=cpar, epar=epar, lv=lv, hp=hp:
                         s.activation(sE[:, epar, :, lv:512], pS[:, cpar, :, lv:512],
                                      AF.Exp, bias=sNB[:, hp:hp + 1]),
                         wait=wexp)
                    exp_hist[g] = P.mark("s", "EXPD")
                    if diag:
                        P.st("v", lambda v, epar=epar, lv=lv:
                             v.tensor_tensor(sE[:, epar, :, lv:lv + 128],
                                             sE[:, epar, :, lv:lv + 128], sTRI[:], ALU.mult),
                             wait=[exp_hist[g]])
                        exp_hist[g] = P.mark("v", "TRID")
                pending_evac.append((gidx - 1, hp, J))
        while pend:
            emit_pv(pend.pop(0))
        while pending_evac or pending_bnd:
            if pending_bnd:
                prev_trid_tmpa, prev_atm_evc = pending_bnd.pop(0)()
            else:
                emit_evac(*pending_evac.pop(0))
        emit_sq_flush()
        mu3 = sSCR[:, :, 2:3].rearrange("p t o -> p (t o)")
        c30 = sSCR[:, :, 0:1].rearrange("p t o -> p (t o)")
        c31 = sSCR[:, :, 1:2].rearrange("p t o -> p (t o)")
        s1f = sMV4[:, :, 0:1].rearrange("p t o -> p (t o)")
        s2f = sMV4[:, :, 1:2].rearrange("p t o -> p (t o)")
        rs3 = sRS[:, :, 0:1].rearrange("p t o -> p (t o)")
        nm3 = sRS[:, :, 1:2].rearrange("p t o -> p (t o)")

        def stats3(v):
            v.tensor_scalar(mu3, s1f, 1.0 / 1024.0, None, ALU.mult)
            v.tensor_tensor(c30, mu3, mu3, ALU.mult)
            v.tensor_scalar(c31, s2f, 1.0 / 1024.0, None, ALU.mult)
            v.tensor_tensor(c31, c31, c30, ALU.subtract)
            v.tensor_scalar(c31, c31, EPS, None, ALU.add)
            return v.reciprocal(c30, c31)
        P.st("v", stats3, wait=[prev_sq[0], prev_rr[0]])
        a30 = P.mark("v", "AUX")
        P.st("s", lambda s: s.activation(rs3, c30, AF.Sqrt), wait=[a30])
        a31 = P.mark("s", "AUX")

        def newton3(v):
            v.tensor_tensor(c30, rs3, rs3, ALU.mult)
            v.tensor_tensor(c30, c30, c31, ALU.mult)
            v.tensor_scalar(c30, c30, -0.5, 1.5, ALU.mult, ALU.add)
            v.tensor_tensor(rs3, rs3, c30, ALU.mult)
            v.tensor_tensor(nm3, mu3, rs3, ALU.mult)
            return v.tensor_scalar(nm3, nm3, -1.0, None, ALU.mult)
        P.st("v", newton3, wait=[a31])
        aux3 = P.mark("v", "AUX")
        P.barrier()
        sc_es.close()
        qk_es.close()
        if _acut <= 6:
            aes.close()
            return
        wo_es = ExitStack()
        sWO = sbuf(wo_es, "sWO", [128, NC, D], BF16)
        sBETC = sbuf(wo_es, "sBETC", [128, 8], BF16)
        sBOR = sbuf(wo_es, "sBOR", [1, D], F32)
        sBOW = sbuf(wo_es, "sBOW", [1, D], BF16)
        sONE1B = sbuf(wo_es, "sONE1B", [1, 128], BF16)
        pbw_es = ExitStack()
        pBW = psum(pbw_es, "pBW", [1, D], F32)
        dma("g", sWO[:], d_wo.rearrange("(c p) n -> p c n", p=128), sem="LGO")
        dma("g", sBETC[:], d_betf.rearrange("(c p) -> p c", p=128), sem="LGO",
            allow_slow_non_contiguous=True)
        ldg_wo = ("LGO", P.val["LGO"])
        dma("y", sBOR[:], d_bo, sem="LDBO")
        ld_bo = ("LDBO", P.val["LDBO"])
        P.st("v", lambda v: v.memset(sONE1B[:], 1.0))
        P.mark("v", "AUX")

        def fbw(t):
            mi = None
            for s2 in range(2):
                for c in range(NC):
                    mi = t.matmul(pBW[0:1, 512 * s2:512 * (s2 + 1)],
                                  sBETC[:, c:c + 1],
                                  sWO[:, c, 512 * s2:512 * (s2 + 1)],
                                  start=(c == 0), stop=(c == NC - 1))
            return mi
        P.st("t", fbw, wait=[ldg_wo, P.cur("AUX3_v")])
        a_bw = P.mark("t", "AUX")
        P.st("v", lambda v: v.tensor_tensor(sBOW[:], sBOR[:], pBW[:], ALU.add),
             wait=[a_bw, ld_bo])
        a_bw2 = P.mark("v", "AUX")
        for c in range(NC):
            P.st("s", lambda s, c=c:
                 s.activation(sWO[:, c, :], sWO[:, c, :], AF.Identity,
                              scale=sBIAS[:, 22 + c:23 + c]),
                 wait=[a_bw2] if c == 0 else [])
        a_wo = P.mark("s", "AUX")
        pbw_es.close()

        ln3es = ExitStack()
        lnv3 = emit_ln_transpose(ln3es, lambda t: sANBv[:, t], sRS,
                                 [aux3, prev_sq[0]], src3d=True)

        pw_es = ExitStack()
        pW = psum(pw_es, "pW", [128, 2, D], F32)
        res_hist = []
        for t in range(NT):
            par = t & 1
            def fwo(tt, t=t, par=par):
                mi = None
                for s2 in range(2):
                    tt.matmul(pW[:, par, 512 * s2:512 * (s2 + 1)], sONE1B[:],
                              sBOW[0:1, 512 * s2:512 * (s2 + 1)], start=True, stop=False)
                    for c in range(NC):
                        mi = tt.matmul(pW[:, par, 512 * s2:512 * (s2 + 1)],
                                       sT[:, c, 128 * t:128 * (t + 1)],
                                       sWO[:, c, 512 * s2:512 * (s2 + 1)],
                                       start=False, stop=(c == NC - 1))
                return mi
            w = [a_wo, lnv3[t]] if t == 0 else [lnv3[t]]
            if t >= 2:
                w.append(res_hist[t - 2])
            P.st("t", fwo, wait=w)
            a_t = P.mark("t", "AUX")
            P.st("v", lambda v, t=t, par=par:
                 v.tensor_tensor(sX[:, t, :], sX[:, t, :], pW[:, par, :], ALU.add),
                 wait=[a_t])
            res_hist.append(P.mark("v", "EVC"))
            P.st("v", lambda v, t=t: v.bn_stats(sSTAT[:, t, 0:6], sX[:, t, 0:512]),
                 wait=[res_hist[-1]])
            P.st("v", lambda v, t=t: v.bn_stats(sSTAT[:, t, 6:12], sX[:, t, 512:1024]))
        bn_mark = P.mark("v", "AUX")
        for t in range(NT):
            P.st("v", lambda v, t=t: v.bn_aggr(sMV4[:, t, 0:2], sSTAT[:, t, 0:6]),
                 wait=[bn_mark] if t == 0 else [])
            P.st("v", lambda v, t=t: v.bn_aggr(sMV4[:, t, 2:4], sSTAT[:, t, 6:12]))
        post_ag[0] = P.mark("v", "AUX")
        P.barrier()
        pw_es.close()
        ln3es.close()
        wo_es.close()
        aes.close()

    import os as _os
    _nph = int(_os.environ.get("KERN_PHASES", "3"))
    _phases = [lambda: emit_ffn(first=True, post_stats=True), emit_attn,
               lambda: emit_ffn(pre_ag=post_ag[0], store_out=True)]
    for _f in _phases[:_nph]:
        _f()

    if _nph == 3:
        P.st("g", None, wait=[("ST", 256)])
    else:
        final_snap = list(P.last_snap)
        P.st("y", lambda e: e.dma_start(out=dy.rearrange("(i p) f -> p i f", p=128), in_=sX[:]),
             wait=final_snap,
             inc=[("ST", 16)])
        P.st("g", None, wait=[("ST", 16)])

    sems = P.sem
    steps = P.steps

    OPS = {"tensor_scalar", "tensor_tensor", "tensor_copy", "reciprocal",
           "reduce_max", "reduce_sum", "bn_stats", "bn_aggr",
           "scalar_tensor_tensor", "memset", "activation", "square",
           "copy", "mul", "add", "sign", "tensor_single_scalar", "select"}

    import os as _os2
    _DRAINS = _os2.environ.get("KERN_NODRAIN", "") != "1"

    class DrainProxy:
        def __init__(self, eng, sem, ctr):
            self._e = eng
            self._sem = sem
            self._ctr = ctr
            self._last = None

        def __getattr__(self, name):
            a = getattr(self._e, name)
            if name in OPS and _DRAINS:
                def wrapped(*args, **kw):
                    if self._last is not None:
                        self._ctr[0] += 1
                        self._last.then_inc(self._sem, 1)
                        self._e.wait_ge(self._sem, self._ctr[0])
                    self._last = a(*args, **kw)
                    return self._last
                return wrapped
            return a

    for s in ("SD_v", "SD_s"):
        P.new_sem(s)

    def run_for(eng_obj, tag):
        use_proxy = tag in ("v", "s")
        ctr = [0]
        for stp in steps:
            if stp["eng"] != tag:
                continue
            for (s, v) in stp["waits"]:
                eng_obj.wait_ge(sems[s], v)
            if stp["fn"] is not None:
                eo = DrainProxy(eng_obj, sems["SD_" + tag], ctr) if use_proxy else eng_obj
                instr = stp["fn"](eo)
                assert instr is not None, f"step on {tag} returned no instruction"
                for (s, n) in stp["incs"]:
                    instr.then_inc(sems[s], n)

    with es:
        with nc.Block() as block:
            @block.sync
            def _(sync):
                run_for(sync, "y")

            @block.tensor
            def _(tensor):
                run_for(tensor, "t")

            @block.vector
            def _(vector):
                run_for(vector, "v")

            @block.scalar
            def _(scalar):
                run_for(scalar, "s")

            @block.gpsimd
            def _(gpsimd):
                run_for(gpsimd, "g")

    return nc


_NC_CACHE = {}


def _get_nc():
    import os as _os
    key = "nc" + _os.environ.get("KERN_PHASES", "3") + "." + _os.environ.get("KERN_FFNCUT", "99") + "." + _os.environ.get("KERN_ATTCUT", "99")
    if key not in _NC_CACHE:
        _NC_CACHE[key] = build_program()
    return _NC_CACHE[key]


def kernel(**inputs):
    f32 = np.float32
    x = np.asarray(inputs["L0_attn_x"], f32)
    B = x.shape[0]
    sw1 = np.asarray(inputs["L0_ffn_spline_w1"], f32)
    sb1 = np.asarray(inputs["L0_ffn_spline_b1"], f32)
    sw2 = np.asarray(inputs["L0_ffn_spline_w2"], f32)
    w2bd = np.zeros((D, 64), f32)
    for tt in range(64):
        w2bd[tt * 16:(tt + 1) * 16, tt] = sw2[tt, :, 0]
    dirs = np.ascontiguousarray(np.asarray(inputs["L0_ffn_directions"], f32))

    tri = np.zeros((128, 128), f32)
    for m in range(128):
        tri[m, m:] = 1.0
    tri2 = np.ascontiguousarray(np.stack([tri, tri], axis=1)).astype(ml_dtypes.bfloat16)
    prio = np.ascontiguousarray(np.broadcast_to(np.arange(8, 0, -1, dtype=f32), (128, 8)))

    lnb = np.asarray(inputs["L0_attn_ln_b"], f32)
    common = dict(
        wq=np.ascontiguousarray(np.asarray(inputs["L0_attn_Wq"], f32)),
        wk=np.ascontiguousarray(np.asarray(inputs["L0_attn_Wk"], f32)),
        wv=np.ascontiguousarray(np.asarray(inputs["L0_attn_Wv"], f32)),
        wo=np.ascontiguousarray(np.asarray(inputs["L0_attn_Wo"], f32)),
        bq=np.ascontiguousarray(np.asarray(inputs["L0_attn_bq"], f32)),
        bk=np.ascontiguousarray(np.asarray(inputs["L0_attn_bk"], f32)),
        bv=np.ascontiguousarray(np.asarray(inputs["L0_attn_bv"], f32)),
        bo=np.ascontiguousarray(np.asarray(inputs["L0_attn_bo"], f32).reshape(1, D)),
        lng=np.ascontiguousarray(np.asarray(inputs["L0_attn_ln_g"], f32)),
        betf=np.ascontiguousarray(lnb.reshape(D)),
        nbet=np.ascontiguousarray(np.asarray(inputs["L0_ffn_norm_beta"], f32).reshape(1, D)),
        sigsT=np.ascontiguousarray(np.asarray(inputs["L0_ffn_signatures_raw"], f32).T),
        knots=np.ascontiguousarray(np.asarray(inputs["L0_ffn_knot_values"], f32).reshape(1, 8)),
        temp=np.ascontiguousarray(np.asarray(inputs["L0_ffn_temperature"], f32).reshape(1, 1)),
        cw1=np.ascontiguousarray(np.asarray(inputs["L0_ffn_compress_w1"], f32)),
        cb1=np.ascontiguousarray(np.asarray(inputs["L0_ffn_compress_b1"], f32)),
        cw2=np.ascontiguousarray(np.asarray(inputs["L0_ffn_compress_w2"], f32)),
        cb2=np.ascontiguousarray(np.asarray(inputs["L0_ffn_compress_b2"], f32)),
        w1f=np.ascontiguousarray(sw1.transpose(1, 0, 2).reshape(2, D)),
        b1f=np.ascontiguousarray(sb1.reshape(D)),
        w2bd=w2bd,
        b2sp=np.ascontiguousarray(np.asarray(inputs["L0_ffn_spline_b2"], f32)[:, 0]),
        dirs=dirs,
        dirsT=np.ascontiguousarray(dirs.T),
        gam=np.ascontiguousarray(np.asarray(inputs["L0_ffn_norm_gamma"], f32).reshape(1, D)),
        osc=np.ascontiguousarray(np.asarray(inputs["L0_ffn_output_scale"], f32).reshape(1, 1)),
        idf=np.eye(128, dtype=f32),
        idb=np.eye(128, dtype=f32).astype(ml_dtypes.bfloat16),
        tri=tri2,
        negi=(np.eye(128, dtype=f32) * -1e10).astype(ml_dtypes.bfloat16),
        tril=np.ascontiguousarray(1.0 - tri).astype(ml_dtypes.bfloat16),
        prio=prio,
        onesf=np.ones((1, 128), f32),
        onesqb=np.ones((128, 128), f32).astype(ml_dtypes.bfloat16),
    )

    nc = _get_nc()
    in_maps = []
    for b in range(B):
        m = dict(common)
        m["x"] = np.ascontiguousarray(x[b])
        in_maps.append(m)
    res = run_bass_kernel_spmd(nc, in_maps, core_ids=list(range(B)))
    out = np.stack([res.results[b]["y"] for b in range(B)], axis=0)
    return out.astype(np.float32)


# version 23 (final)
# speedup vs baseline: 1.0241x; 1.0241x over previous
import numpy as np
import ml_dtypes
import concourse.bass as bass
import concourse.mybir as mybir
from concourse.bass_utils import run_bass_kernel_spmd
from contextlib import ExitStack

F32 = mybir.dt.float32
BF16 = mybir.dt.bfloat16
AF = mybir.ActivationFunctionType
ALU = mybir.AluOpType
AX = mybir.AxisListType

NT = 16
D = 1024
N = 2048
NC = 8
EPS = 1e-5
ENGS = ("t", "v", "s", "g", "y")


def bcast0(ap, dim, count):
    dims = [list(d) for d in ap.ap]
    dims[dim] = [0, count]
    return bass.AP(ap.tensor, ap.offset, dims)


class Plan:
    def __init__(self, nc, es):
        self.nc = nc
        self.es = es
        self.steps = []
        self.sem = {}
        self.val = {}
        self.dirty = {e: False for e in ENGS}
        self.last_step = {e: None for e in ENGS}
        self.last_snap = []

    def new_sem(self, name):
        if name in self.sem:
            return
        self.sem[name] = self.es.enter_context(self.nc.semaphore(name))
        self.val[name] = 0

    def st(self, eng, fn=None, wait=(), inc=()):
        waits = []
        for w in wait:
            if isinstance(w, tuple):
                if w[1] > 0:
                    waits.append(w)
            else:
                if self.val[w] > 0:
                    waits.append((w, self.val[w]))
        incs = []
        for name, n in inc:
            self.val[name] += n
            incs.append((name, n))
        step = dict(eng=eng, fn=fn, waits=waits, incs=incs,
                    post={name: self.val[name] for (name, n) in incs})
        self.steps.append(step)
        if fn is not None:
            self.dirty[eng] = True
            self.last_step[eng] = step
        return step

    def mark(self, eng, sname, n=1):
        key = sname + "_" + eng
        self.new_sem(key)
        stp = self.last_step[eng]
        assert stp is not None and stp["fn"] is not None, f"mark {eng} {sname}"
        self.val[key] += n
        stp["incs"].append((key, n))
        stp["post"][key] = self.val[key]
        return (key, self.val[key])

    def cur(self, key):
        return (key, self.val.get(key, 0))

    def barrier(self, engs=("t", "v", "s", "g")):
        snap = []
        for e in engs:
            if not self.dirty[e]:
                continue
            stp = self.last_step[e]
            if stp["incs"]:
                sname, _ = stp["incs"][-1]
                snap.append((e, sname, stp["post"][sname]))
            else:
                key, v = self.mark(e, "B", 16 if e in ("g", "y") else 1)
                snap.append((e, key, v))
            self.dirty[e] = False
        for e in engs:
            waits = [(s, v) for (eo, s, v) in snap if v > 0]
            if waits:
                self.st(e, None, wait=waits)
        self.last_snap = [(s, v) for (_, s, v) in snap]
        return self.last_snap


def build_program():
    nc = bass.Bass("TRN2", target_bir_lowering=False, debug=False)

    dx = nc.dram_tensor("x", [N, D], F32, kind="ExternalInput").ap()
    dy = nc.dram_tensor("y", [N, D], F32, kind="ExternalOutput").ap()
    d_wq = nc.dram_tensor("wq", [D, D], F32, kind="ExternalInput").ap()
    d_wk = nc.dram_tensor("wk", [D, 256], F32, kind="ExternalInput").ap()
    d_wv = nc.dram_tensor("wv", [D, 256], F32, kind="ExternalInput").ap()
    d_wo = nc.dram_tensor("wo", [D, D], F32, kind="ExternalInput").ap()
    d_bq = nc.dram_tensor("bq", [D], F32, kind="ExternalInput").ap()
    d_bk = nc.dram_tensor("bk", [256], F32, kind="ExternalInput").ap()
    d_bv = nc.dram_tensor("bv", [256], F32, kind="ExternalInput").ap()
    d_bo = nc.dram_tensor("bo", [1, D], F32, kind="ExternalInput").ap()
    d_lng = nc.dram_tensor("lng", [D], F32, kind="ExternalInput").ap()
    d_betf = nc.dram_tensor("betf", [D], F32, kind="ExternalInput").ap()
    d_nbet = nc.dram_tensor("nbet", [1, D], F32, kind="ExternalInput").ap()
    d_sigsT = nc.dram_tensor("sigsT", [D, 64], F32, kind="ExternalInput").ap()
    d_knots = nc.dram_tensor("knots", [1, 8], F32, kind="ExternalInput").ap()
    d_temp = nc.dram_tensor("temp", [1, 1], F32, kind="ExternalInput").ap()
    d_cw1 = nc.dram_tensor("cw1", [D, 256], F32, kind="ExternalInput").ap()
    d_cb1 = nc.dram_tensor("cb1", [256], F32, kind="ExternalInput").ap()
    d_cw2 = nc.dram_tensor("cw2", [256, 2], F32, kind="ExternalInput").ap()
    d_cb2 = nc.dram_tensor("cb2", [2], F32, kind="ExternalInput").ap()
    d_w1f = nc.dram_tensor("w1f", [2, D], F32, kind="ExternalInput").ap()
    d_b1f = nc.dram_tensor("b1f", [D], F32, kind="ExternalInput").ap()
    d_w2bd = nc.dram_tensor("w2bd", [D, 64], F32, kind="ExternalInput").ap()
    d_b2sp = nc.dram_tensor("b2sp", [64], F32, kind="ExternalInput").ap()
    d_dirs = nc.dram_tensor("dirs", [64, D], F32, kind="ExternalInput").ap()
    d_dirsT = nc.dram_tensor("dirsT", [D, 64], F32, kind="ExternalInput").ap()
    d_gam = nc.dram_tensor("gam", [1, D], F32, kind="ExternalInput").ap()
    d_osc = nc.dram_tensor("osc", [1, 1], F32, kind="ExternalInput").ap()
    d_idf = nc.dram_tensor("idf", [128, 128], F32, kind="ExternalInput").ap()
    d_idb = nc.dram_tensor("idb", [128, 128], BF16, kind="ExternalInput").ap()
    d_tri = nc.dram_tensor("tri", [128, 2, 128], BF16, kind="ExternalInput").ap()
    d_negi = nc.dram_tensor("negi", [128, 128], BF16, kind="ExternalInput").ap()
    d_tril = nc.dram_tensor("tril", [128, 128], BF16, kind="ExternalInput").ap()
    d_prio = nc.dram_tensor("prio", [128, 8], F32, kind="ExternalInput").ap()
    d_onesf = nc.dram_tensor("onesf", [1, 128], F32, kind="ExternalInput").ap()
    d_onesqb = nc.dram_tensor("onesqb", [128, 128], BF16, kind="ExternalInput").ap()

    es = ExitStack()
    P = Plan(nc, es)
    for s in ["LD", "LDG", "ST"]:
        P.new_sem(s)

    _uid = [0]

    def sbuf(stack, name, shape, dt, side=None):
        _uid[0] += 1
        return stack.enter_context(nc.sbuf_tensor(f"{name}_{_uid[0]}", shape, dt, side=side))

    def psum(stack, name, shape, dt):
        _uid[0] += 1
        return stack.enter_context(nc.psum_tensor(f"{name}_{_uid[0]}", shape, dt))

    sX = es.enter_context(nc.sbuf_tensor("sX", [128, NT, D], F32))
    sT = es.enter_context(nc.sbuf_tensor("sT", [128, NC, N], BF16))
    sIDF = es.enter_context(nc.sbuf_tensor("sIDF", [128, 128], F32))
    sIDB = es.enter_context(nc.sbuf_tensor("sIDB", [128, 128], BF16))
    sTRI = es.enter_context(nc.sbuf_tensor("sTRI", [128, 2, 128], BF16))
    sNEGI = es.enter_context(nc.sbuf_tensor("sNEGI", [128, 128], BF16))
    sTRIL = es.enter_context(nc.sbuf_tensor("sTRIL", [128, 128], BF16))
    sPRIO = es.enter_context(nc.sbuf_tensor("sPRIO", [128, 8], F32))
    sONESF = es.enter_context(nc.sbuf_tensor("sONESF", [1, 128], F32))
    sONESQB = es.enter_context(nc.sbuf_tensor("sONESQB", [128, 128], BF16))
    sROWS = es.enter_context(nc.sbuf_tensor("sROWS", [1, 16], F32))
    sKN = es.enter_context(nc.sbuf_tensor("sKN", [128, 36], F32))
    sGB = es.enter_context(nc.sbuf_tensor("sGB", [128, 2, D], BF16))
    sSTAT = es.enter_context(nc.sbuf_tensor("sSTAT", [128, NT, 12], F32))
    sMV4 = es.enter_context(nc.sbuf_tensor("sMV4", [128, NT, 4], F32))
    sRS = es.enter_context(nc.sbuf_tensor("sRS", [128, NT, 2], F32))
    sRS2 = es.enter_context(nc.sbuf_tensor("sRS2", [128, NT, 2], F32))
    sSCR = es.enter_context(nc.sbuf_tensor("sSCR", [128, NT, 4], F32))
    sBIAS = es.enter_context(nc.sbuf_tensor("sBIAS", [128, 32], F32))
    sCB2 = es.enter_context(nc.sbuf_tensor("sCB2", [2, 1], F32))
    sB2SP = es.enter_context(nc.sbuf_tensor("sB2SP", [64, 1], F32))
    sXFB = es.enter_context(nc.sbuf_tensor("sXFB", [128, 2, D], BF16))
    sGC = es.enter_context(nc.sbuf_tensor("sGC", [128, 8], F32))
    sSG = es.enter_context(nc.sbuf_tensor("sSG", [128, 8, 72], BF16))
    sDSB = es.enter_context(nc.sbuf_tensor("sDSB", [128, 2, 64], F32))
    sDG = es.enter_context(nc.sbuf_tensor("sDG", [128, NC, 128], BF16))
    sGBr = es.enter_context(nc.sbuf_tensor("sGBr", [2, D], BF16))

    def dma(q, out, in_, sem=None, **kw):
        if sem is None:
            sem = "LD" if q == "y" else "LDG"
        P.new_sem(sem)
        P.st(q, lambda e, o=out, i=in_, kw=kw: e.dma_start(out=o, in_=i, **kw),
             wait=list(P.last_snap), inc=[(sem, 16)])
        return P.val[sem]

    setup_es = ExitStack()
    sGRow = sbuf(setup_es, "sGRow", [1, 2, D], F32)
    pKN = psum(setup_es, "pKN", [128, 16], F32)
    pGB = psum(setup_es, "pGB", [128, D], F32)

    dma("y", sIDF[:], d_idf)
    dma("y", sIDB[:], d_idb)
    dma("y", sTRI[:], d_tri)
    dma("y", sNEGI[:], d_negi)
    dma("y", sTRIL[:], d_tril)
    dma("y", sPRIO[:], d_prio)
    dma("y", sONESF[:], d_onesf)
    dma("y", sONESQB[:], d_onesqb)
    dma("y", sROWS[0:1, 0:8], d_knots)
    dma("y", sROWS[0:1, 8:9], d_temp)
    dma("y", sROWS[0:1, 9:10], d_osc)
    dma("y", sGRow[0:1, 0, :], d_gam)
    dma("y", sGRow[0:1, 1, :], d_nbet)
    dma("y", sBIAS[:, 0:8], d_bq.rearrange("(c p) -> p c", p=128), allow_slow_non_contiguous=True)
    dma("y", sBIAS[:, 8:10], d_bk.rearrange("(c p) -> p c", p=128), allow_slow_non_contiguous=True)
    dma("y", sBIAS[:, 10:12], d_bv.rearrange("(c p) -> p c", p=128), allow_slow_non_contiguous=True)
    dma("y", sBIAS[:, 12:14], d_cb1.rearrange("(c p) -> p c", p=128), allow_slow_non_contiguous=True)
    dma("y", sBIAS[:, 14:22], d_b1f.rearrange("(c p) -> p c", p=128), allow_slow_non_contiguous=True)
    dma("y", sBIAS[:, 22:30], d_lng.rearrange("(c p) -> p c", p=128), allow_slow_non_contiguous=True)
    dma("y", sCB2[:], d_cb2.rearrange("(a b) -> a b", b=1))
    dma("y", sB2SP[:], d_b2sp.rearrange("(a b) -> a b", b=1))
    dma("y", sGC[:], d_gam.rearrange("o (c p) -> p (o c)", p=128),
        allow_slow_non_contiguous=True)
    LD_setup = P.val["LD"]
    xld = []
    for q4 in range(4):
        dma("y", sX[:, 4 * q4:4 * (q4 + 1), :],
            dx.rearrange("(i p) f -> p i f", p=128)[:, 4 * q4:4 * (q4 + 1), :],
            sem=f"LDX{q4}")
        xld.append((f"LDX{q4}", 16))


    def setup_small(v):
        v.tensor_scalar(sBIAS[:, 0:8], sBIAS[:, 0:8], 0.125, None, ALU.mult)
        v.tensor_scalar(sROWS[0:1, 10:11], sROWS[0:1, 8:9], 1e-6, None, ALU.add)
        return v.reciprocal(sROWS[0:1, 11:12], sROWS[0:1, 10:11])
    P.st("v", setup_small, wait=[("LD", LD_setup)])
    a_small = P.mark("v", "AUX")

    def mm_kn(t):
        t.matmul(pKN[:, 0:8], sONESF[:], sROWS[0:1, 0:8], start=True, stop=True)
        t.matmul(pKN[:, 8:9], sONESF[:], sROWS[0:1, 11:12], start=True, stop=True)
        return t.matmul(pKN[:, 9:10], sONESF[:], sROWS[0:1, 9:10], start=True, stop=True)
    P.st("t", mm_kn, wait=[a_small, ("LD", LD_setup)])
    a_kn = P.mark("t", "AUX")

    def kn_evac(v):
        v.tensor_copy(sKN[:, 0:8], pKN[:, 0:8])
        v.tensor_scalar(sKN[:, 16:17], pKN[:, 8:9], -1.0, None, ALU.mult)
        v.tensor_copy(sKN[:, 17:18], pKN[:, 9:10])
        v.tensor_tensor(sKN[:, 8:15], sKN[:, 1:8], sKN[:, 0:7], ALU.subtract)
        v.tensor_copy(sKN[:, 20:21], sKN[:, 8:9])
        v.tensor_tensor(sKN[:, 21:27], sKN[:, 9:15], sKN[:, 8:14], ALU.subtract)
        return v.tensor_scalar(sKN[:, 28:35], sPRIO[:, 0:7], -8.0, None, ALU.add)
    P.st("v", kn_evac, wait=[a_kn])
    P.mark("v", "AUX")

    def mk_gb(idx):
        def f(t):
            mi = None
            for half in range(2):
                mi = t.matmul(pGB[:, 512 * half:512 * (half + 1)], sONESF[:],
                              sGRow[0:1, idx, 512 * half:512 * (half + 1)],
                              start=True, stop=True)
            return mi
        return f
    P.st("t", mk_gb(0), wait=[("LD", LD_setup)])
    a_g0 = P.mark("t", "AUX")
    P.st("s", lambda s: s.activation(sGB[:, 0, :], pGB[:], AF.Copy, scale=0.5),
         wait=[a_g0])
    a_g1 = P.mark("s", "AUX")
    P.st("t", mk_gb(1), wait=[a_g1])
    a_g2 = P.mark("t", "AUX")
    P.st("s", lambda s: s.activation(sGB[:, 1, :], pGB[:], AF.Copy, scale=0.5),
         wait=[a_g2])
    P.mark("s", "AUX")

    sGCB = sbuf(setup_es, "sGCB", [128, 8], BF16)
    def mk_dg(v):
        v.tensor_scalar(sGCB[:], sGC[:], 0.5, None, ALU.mult)
        mi = None
        for c in range(NC):
            mi = v.tensor_tensor(sDG[:, c, :], sIDB[:], bcast0(sGCB[:, c:c + 1], 1, 128),
                                 ALU.mult)
        return mi
    P.st("v", mk_dg, wait=[("LD", LD_setup)])
    P.mark("v", "AUX")
    P.barrier()
    dma("y", sGBr[0:1, :], sGB[0:1, 0, :], sem="LDGB")
    dma("y", sGBr[1:2, :], sGB[0:1, 1, :], sem="LDGB")
    ldgb = ("LDGB", P.val["LDGB"])
    setup_es.close()

    def emit_ln_stats(src_tiles, rs_out, tile_waits=None):
        for t in range(NT):
            ap = src_tiles(t)
            tw = tile_waits[t] if tile_waits else []
            P.st("v", lambda v, t=t, ap=ap: v.bn_stats(sSTAT[:, t, 0:6], ap[:, 0:512]), wait=tw)
            P.st("v", lambda v, t=t, ap=ap: v.bn_stats(sSTAT[:, t, 6:12], ap[:, 512:1024]))
        bn_mark = P.mark("v", "AUX")
        for t in range(NT):
            P.st("v", lambda v, t=t: v.bn_aggr(sMV4[:, t, 0:2], sSTAT[:, t, 0:6]),
                 wait=[bn_mark] if t == 0 else [])
            P.st("v", lambda v, t=t: v.bn_aggr(sMV4[:, t, 2:4], sSTAT[:, t, 6:12]))
        ag_mark = P.mark("v", "AUX")
        return emit_ln_stats_tail(rs_out, [ag_mark])

    def emit_ln_stats_tail(rs_out, wait0):
        m1 = sMV4[:, :, 0:1].rearrange("p t o -> p (t o)")
        v1 = sMV4[:, :, 1:2].rearrange("p t o -> p (t o)")
        m2 = sMV4[:, :, 2:3].rearrange("p t o -> p (t o)")
        v2 = sMV4[:, :, 3:4].rearrange("p t o -> p (t o)")
        mu = sSCR[:, :, 2:3].rearrange("p t o -> p (t o)")
        sc0 = sSCR[:, :, 0:1].rearrange("p t o -> p (t o)")
        sc1 = sSCR[:, :, 1:2].rearrange("p t o -> p (t o)")
        rs = rs_out[:, :, 0:1].rearrange("p t o -> p (t o)")
        nm = rs_out[:, :, 1:2].rearrange("p t o -> p (t o)")

        def stats_math(v):
            v.tensor_tensor(mu, m1, m2, ALU.add)
            v.tensor_scalar(mu, mu, 0.5, None, ALU.mult)
            v.tensor_tensor(sc0, m1, m2, ALU.subtract)
            v.tensor_tensor(sc0, sc0, sc0, ALU.mult)
            v.tensor_tensor(sc1, v1, v2, ALU.add)
            v.tensor_scalar(sc1, sc1, 0.5, None, ALU.mult)
            v.scalar_tensor_tensor(sc1, sc0, 0.25, sc1, ALU.mult, ALU.add)
            v.tensor_scalar(sc1, sc1, EPS, None, ALU.add)
            return v.reciprocal(sc0, sc1)
        P.st("v", stats_math, wait=wait0 if wait0 else [])
        a0 = P.mark("v", "AUX")
        P.st("s", lambda s: s.activation(rs, sc0, AF.Sqrt), wait=[a0])
        a1 = P.mark("s", "AUX")

        def newton(v):
            v.tensor_tensor(sc0, rs, rs, ALU.mult)
            v.tensor_tensor(sc0, sc0, sc1, ALU.mult)
            v.tensor_scalar(sc0, sc0, -0.5, 1.5, ALU.mult, ALU.add)
            v.tensor_tensor(rs, rs, sc0, ALU.mult)
            v.tensor_tensor(nm, mu, rs, ALU.mult)
            return v.tensor_scalar(nm, nm, -1.0, None, ALU.mult)
        P.st("v", newton, wait=[a1])
        return P.mark("v", "AUX")

    def emit_ln_transpose(stack, src_tiles, rs, aux_ready, src3d=False):
        pTR = psum(stack, "pTR", [128, 3, NC, 128], BF16)
        lnt_hist = []
        lnv_hist = []
        for t in range(NT):
            par = t % 2
            ap = src_tiles(t)
            xout = (sXFB[:, par, :].rearrange("p (c q) -> p c q", q=128)
                    if src3d else sXFB[:, par, :])
            w = list(aux_ready) if isinstance(aux_ready, list) else [aux_ready]
            if t >= 2:
                w.append(lnt_hist[t - 2])
            P.st("s", lambda s, xout=xout, ap=ap, t=t:
                 s.activation(xout, ap, AF.Identity,
                              bias=rs[:, t, 1:2], scale=rs[:, t, 0:1]),
                 wait=w)
            lns = P.mark("s", "LNS")
            par3 = t % 3
            for c in range(NC):
                wt = [lns]
                if t >= 3 and c == 0:
                    wt.append(lnv_hist[t - 3])
                P.st("t", lambda tt, par=par, par3=par3, c=c:
                     tt.transpose(pTR[:, par3, c, :], sXFB[:, par, 128 * c:128 * (c + 1)],
                                  sIDB[:]),
                     wait=wt)
            lnt_hist.append(P.mark("t", "LNT"))
            P.st("v", lambda v, t=t, par3=par3:
                 v.tensor_copy(sT[:, 0:NC, 128 * t:128 * (t + 1)], pTR[:, par3]),
                 wait=[lnt_hist[-1]])
            lnv_hist.append(P.mark("v", "LNV"))
        return lnv_hist

    def emit_ffn(first=False, pre_ag=None, post_stats=False, store_out=False):
        import os as _os
        _fcut = int(_os.environ.get("KERN_FFNCUT", "99"))
        fes = ExitStack()
        sW1C = sbuf(fes, "sW1C", [128, NC, 256], BF16)
        sW2C = sbuf(fes, "sW2C", [128, 2, 2], BF16)
        sW1F = sbuf(fes, "sW1F", [2, D], BF16)
        sW2BD = sbuf(fes, "sW2BD", [128, NC, 64], BF16)
        sDIR = sbuf(fes, "sDIR", [64, D], BF16)
        sDIRT = sbuf(fes, "sDIRT", [128, NC, 64], BF16)
        sCA = sbuf(fes, "sCA", [128, NT * 72], F32)
        sCB = sbuf(fes, "sCB", [128, NT * 72], F32)
        sVAL = sbuf(fes, "sVAL", [128, NT, 72], F32)
        sGT = sbuf(fes, "sGT", [128, NT, 72], F32)
        sCOH = sbuf(fes, "sCOH", [128, NT, 8], F32)
        sLOH = sbuf(fes, "sLOH", [128, NT, 8], F32)
        sSELG = sbuf(fes, "sSELG", [128, NT, 8], F32)
        sTMP8 = sbuf(fes, "sTMP8", [128, NT, 8], F32)
        sSEL = sbuf(fes, "sSEL", [128, NT, 4], F32)
        scTs = sbuf(fes, "scTs", [72, N], F32)
        sHT = sbuf(fes, "sHT", [128, 2, N], BF16)
        sCT = sbuf(fes, "sCT", [2, N], BF16)
        sHH = sbuf(fes, "sHH", [128, 2, N], BF16)
        sMAGT = sbuf(fes, "sMAGT", [64, N], BF16)
        sDOT = sbuf(fes, "sDOT", [64, N], BF16)
        sMGm = sbuf(fes, "sMGm", [128, NT, 64], BF16)
        sDTm = sbuf(fes, "sDTm", [128, NT, 64], BF16)
        sWSEL = sbuf(fes, "sWSEL", [128, NT, 64], BF16)
        sWT = sbuf(fes, "sWT", [64, N], BF16)
        sSC64 = sbuf(fes, "sSC64", [128, NT, 64], F32)
        sYS = sbuf(fes, "sYS", [128, NT, 4], F32)
        sDS2 = sbuf(fes, "sDS2", [64, 2], F32)
        sDR1 = sbuf(fes, "sDR1", [1, 64], F32)
        sDR2 = sbuf(fes, "sDR2", [1, 64], F32)
        sDIRG = sbuf(fes, "sDIRG", [64, D], BF16)
        sWS = sbuf(fes, "sWS", [128, 32], F32)
        sWSB = sbuf(fes, "sWSB", [128, 32], BF16)
        sRT = sbuf(fes, "sRT", [2, 16, 128], BF16)

        dma("g", sW1C[:], d_cw1.rearrange("(c p) n -> p c n", p=128), sem="LGW")
        dma("g", sW2C[:], d_cw2.rearrange("(c p) n -> p c n", p=128), sem="LGW")
        dma("g", sW1F[:], d_w1f, sem="LGW")
        dma("g", sW2BD[:], d_w2bd.rearrange("(c p) n -> p c n", p=128), sem="LGW")
        dma("g", sDIR[:], d_dirs, sem="LGD")
        ldg_d = ("LGD", P.val["LGD"])
        dma("g", sDIRT[:], d_dirsT.rearrange("(c p) s -> p c s", p=128), sem="LGW")
        ldg_w = ("LGW", P.val["LGW"])
        if first:
            tern_es = ExitStack()
            sSGR = sbuf(tern_es, "sSGR", [128, NC, 64], F32)
            sTMPC = sCA[:, 0:512].rearrange("p (c s) -> p c s", c=NC)
            dma("y", sSGR[:], d_sigsT.rearrange("(c p) s -> p c s", p=128), sem="LDSG")
            ld_sg = ("LDSG", P.val["LDSG"])

        if pre_ag is not None:
            aux = emit_ln_stats_tail(sRS, [pre_ag])
        else:
            tw = [[xld[t // 4]] for t in range(NT)] if first else None
            aux = emit_ln_stats(lambda t: sX[:, t, :], sRS, tile_waits=tw)

        if first:
            def tern(v):
                v.tensor_scalar(sTMPC[:], sSGR[:], 0.3, None, ALU.is_gt)
                v.tensor_scalar(sSGR[:], sSGR[:], -0.3, None, ALU.is_lt)
                v.tensor_tensor(sSG[:, :, 0:64], sTMPC[:], sSGR[:], ALU.subtract)
                v.reduce_sum(sTMPC[:, :, 0:8],
                             sSG[:, :, 0:64].rearrange("p c (l j) -> p c l j", j=8), axis=AX.X)
                v.tensor_scalar(sTMPC[:, :, 8:16], sTMPC[:, :, 0:8], 0.0, None, ALU.is_gt)
                v.tensor_scalar(sTMPC[:, :, 16:24], sTMPC[:, :, 0:8], 0.0, None, ALU.is_lt)
                return v.tensor_tensor(sSG[:, :, 64:72], sTMPC[:, :, 8:16], sTMPC[:, :, 16:24],
                                       ALU.subtract)
            P.st("v", tern, wait=[ld_sg])
            P.mark("v", "AUX")
            tern_es.close()
        if _fcut <= 1:
            P.barrier()
            fes.close()
            return

        lnes = ExitStack()
        lnv = emit_ln_transpose(lnes, lambda t: sX[:, t, :], sRS, aux)

        ses = ExitStack()
        psc = psum(ses, "psc", [72, N], F32)

        if first:
            dd_es = ExitStack()
            sDSQ = sbuf(dd_es, "sDSQ", [64, D], BF16)
            pDR = psum(dd_es, "pDR", [128, 128], F32)
            P.st("s", lambda s: s.activation(sDSQ[:], sDIR[:], AF.Square),
                 wait=[ldg_d])
            a_dsq = P.mark("s", "AUX")

            def dsum(v):
                v.reduce_sum(sDS2[:, 0:1], sDIR[:].rearrange("p (o f) -> p o f", o=1), axis=AX.X)
                return v.reduce_sum(sDS2[:, 1:2], sDSQ[:].rearrange("p (o f) -> p o f", o=1),
                                    axis=AX.X)
            P.st("v", dsum, wait=[a_dsq, ldg_d])
            a_ds = P.mark("v", "AUX")
            def dtr(t):
                t.transpose(pDR[0:1, 0:64], sDS2[:, 0:1], sIDF[0:64, 0:64])
                return t.transpose(pDR[0:1, 64:128], sDS2[:, 1:2], sIDF[0:64, 0:64])
            P.st("t", dtr, wait=[a_ds])
            a_dt = P.mark("t", "AUX")

            def drows(v):
                v.tensor_copy(sDR1[:], pDR[0:1, 0:64])
                return v.tensor_copy(sDR2[:], pDR[0:1, 64:128])
            P.st("v", drows, wait=[a_dt])
            a_dv = P.mark("v", "AUX")
            P.st("t", lambda t: t.matmul(pDR[:, 0:64], sONESF[:], sDR1[:], start=True, stop=True),
                 wait=[a_dv])
            a_b0 = P.mark("t", "AUX")
            P.st("v", lambda v: v.tensor_copy(sDSB[:, 0, :], pDR[:, 0:64]), wait=[a_b0])
            a_b1 = P.mark("v", "AUX")
            P.st("t", lambda t: t.matmul(pDR[:, 0:64], sONESF[:], sDR2[:], start=True, stop=True),
                 wait=[a_b1])
            a_b2 = P.mark("t", "AUX")
            P.st("v", lambda v: v.tensor_copy(sDSB[:, 1, :], pDR[:, 0:64]), wait=[a_b2])
            P.mark("v", "AUX")
            dd_es.close()
        P.st("v", lambda v: v.tensor_tensor(sDIRG[:], sDIR[:], sGB[0:64, 0, :], ALU.mult),
             wait=[ldg_d])
        P.mark("v", "AUX")
        if _fcut <= 3:
            P.barrier()
            fes.close()
            return

        for s4 in range(4):
            def f(t, s4=s4):
                mi = None
                for c in range(NC):
                    mi = t.matmul(psc[:, 512 * s4:512 * (s4 + 1)],
                                  sSG[:, c, :], sT[:, c, 512 * s4:512 * (s4 + 1)],
                                  start=(c == 0), stop=(c == NC - 1))
                return mi
            P.st("t", f, wait=[lnv[4 * s4 + 3]])
        a_sc = P.mark("t", "AUX")
        P.st("v", lambda v: v.tensor_copy(scTs[:], psc[:]), wait=[a_sc])
        sc_evc = P.mark("v", "AUX")
        ses.close()
        lnes.close()
        if _fcut <= 4:
            P.barrier()
            fes.close()
            return

        des = ExitStack()
        pdt = psum(des, "pdt", [64, N], F32)
        for s4 in range(4):
            def f(t, s4=s4):
                mi = None
                for c in range(NC):
                    mi = t.matmul(pdt[:, 512 * s4:512 * (s4 + 1)],
                                  sDIRT[:, c, :], sT[:, c, 512 * s4:512 * (s4 + 1)],
                                  start=(c == 0), stop=(c == NC - 1))
                return mi
            P.st("t", f, wait=[sc_evc, ldg_w] if s4 == 0 else [])
        a_dt2 = P.mark("t", "AUX")
        P.st("s", lambda s: s.activation(sDOT[:], pdt[:], AF.Copy), wait=[a_dt2])
        dot_evc = P.mark("s", "AUX")
        des.close()
        if _fcut <= 5:
            P.barrier()
            fes.close()
            return

        tes = ExitStack()
        pTF = psum(tes, "pTF", [128, 2, 4, 128], F32)
        evc_hist = []
        for g in range(4):
            par = g & 1
            for k in range(4):
                t = 4 * g + k
                w = []
                if g == 0 and k == 0:
                    w.append(dot_evc)
                if g >= 2 and k == 0:
                    w.append(evc_hist[g - 2])
                P.st("t", lambda tt, t=t, par=par, k=k:
                     tt.transpose(pTF[:, par, k, 0:72], scTs[:, 128 * t:128 * (t + 1)],
                                  sIDF[0:72, 0:72]), wait=w)
            trid = P.mark("t", "TRID")
            P.st("v", lambda v, g=g, par=par:
                 v.tensor_copy(sGT[:, 4 * g:4 * (g + 1), :], pTF[:, par, :, 0:72]),
                 wait=[trid])
            evc_hist.append(P.mark("v", "EVC"))
        tes.close()
        if _fcut <= 6:
            P.barrier()
            fes.close()
            return

        gtf = sGT[:].rearrange("p t s -> p (t s)")
        valf = sVAL[:].rearrange("p t s -> p (t s)")
        P.st("v", lambda v: v.tensor_scalar(sCB[:], gtf, sKN[:, 16:17], 87.0, ALU.mult, ALU.min),
             wait=[evc_hist[-1]])
        pre_exp = P.mark("v", "AUX")
        P.st("s", lambda s: s.activation(sCA[:], sCB[:], AF.Exp), wait=[pre_exp])
        a_exp = P.mark("s", "AUX")

        def calib(v):
            v.tensor_scalar(sCB[:], sCA[:], 1.0, None, ALU.add)
            v.reciprocal(sCA[:], sCB[:])
            v.tensor_scalar(sCA[:], sCA[:], 7.0, None, ALU.mult)
            v.tensor_scalar(sCB[:], sCA[:], 1.0, None, ALU.min)
            v.tensor_scalar(valf, sCB[:], sKN[:, 8:9], sKN[:, 0:1], ALU.mult, ALU.add)
            for j in range(1, 7):
                v.tensor_scalar(sCB[:], sCA[:], float(-j), 0.0, ALU.add, ALU.max)
                v.tensor_scalar(sCB[:], sCB[:], 1.0, None, ALU.min)
                v.scalar_tensor_tensor(valf, sCB[:], sKN[:, 8 + j:9 + j], valf, ALU.mult, ALU.add)
            return v.tensor_copy(sSEL[:, :, 0:1], sVAL[:, :, 0:1])
        P.st("v", calib, wait=[a_exp])
        cal_aux = P.mark("v", "AUX")
        if _fcut <= 7:
            P.barrier()
            fes.close()
            return

        def argmax_block(v, src, oh, col):
            v.reduce_max(sSEL[:, :, col], src, axis=AX.X)
            v.tensor_tensor(oh, src, bcast0(sSEL[:, :, col:col + 1], 2, 8), ALU.is_equal)
            v.tensor_tensor(oh, oh,
                            bcast0(sPRIO[:].rearrange("p (o s) -> p o s", o=1), 1, NT), ALU.mult)
            v.reduce_max(sSEL[:, :, col + 1], oh, axis=AX.X)
            return v.tensor_tensor(oh, oh, bcast0(sSEL[:, :, col + 1:col + 2], 2, 8), ALU.is_equal)

        P.st("v", lambda v: argmax_block(v, sVAL[:, :, 64:72], sCOH[:], 0), wait=[cal_aux])
        sel1_aux = P.mark("v", "AUX")

        def select2(v):
            for c in range(8):
                v.tensor_tensor(sTMP8[:], sVAL[:, :, 8 * c:8 * (c + 1)],
                                bcast0(sCOH[:, :, c:c + 1], 2, 8), ALU.mult)
                if c == 0:
                    v.tensor_copy(sSELG[:], sTMP8[:])
                else:
                    v.tensor_tensor(sSELG[:], sSELG[:], sTMP8[:], ALU.add)
            return argmax_block(v, sSELG[:], sLOH[:], 2)
        P.st("v", select2, wait=[sel1_aux])
        sel_aux = P.mark("v", "AUX")
        if _fcut <= 8:
            P.barrier()
            fes.close()
            return

        ces = ExitStack()
        pH = psum(ces, "pH", [128, N], F32)
        for half in range(2):
            for c in range(NC):
                def f(t, c=c, half=half):
                    mi = None
                    for s4 in range(4):
                        mi = t.matmul(pH[:, 512 * s4:512 * (s4 + 1)],
                                      sW1C[:, c, 128 * half:128 * (half + 1)],
                                      sT[:, c, 512 * s4:512 * (s4 + 1)],
                                      start=(c == 0), stop=(c == NC - 1))
                    return mi
                w = []
                if half == 0 and c == 0:
                    w = [evc_hist[-1], ldg_w]
                if half == 1 and c == 0:
                    w = [P.cur("EVC_s")]
                P.st("t", f, wait=w)
            a_h = P.mark("t", "AUX")
            P.st("s", lambda s, half=half:
                 s.activation(sHT[:, half, :], pH[:], AF.Gelu, bias=sBIAS[:, 12 + half:13 + half]),
                 wait=[a_h])
            P.mark("s", "EVC")
        ces.close()
        if _fcut <= 9:
            P.barrier()
            fes.close()
            return

        c2es = ExitStack()
        pC = psum(c2es, "pC", [2, N], F32)
        for c2 in range(2):
            def f(t, c2=c2):
                mi = None
                for s4 in range(4):
                    mi = t.matmul(pC[:, 512 * s4:512 * (s4 + 1)],
                                  sW2C[:, c2, :], sHT[:, c2, 512 * s4:512 * (s4 + 1)],
                                  start=(c2 == 0), stop=(c2 == 1))
                return mi
            P.st("t", f, wait=[P.cur("EVC_s"), ldg_w] if c2 == 0 else [])
        a_c = P.mark("t", "AUX")
        P.st("s", lambda s: s.activation(sCT[:], pC[:], AF.Tanh, bias=sCB2[:]),
             wait=[a_c])
        tanh_evc = P.mark("s", "EVC")
        c2es.close()
        if _fcut <= 10:
            P.barrier()
            fes.close()
            return

        s2es = ExitStack()
        pS1 = psum(s2es, "pS1", [128, N], F32)
        pM = psum(s2es, "pM", [64, N], F32)
        mag_done = []
        relu_done = []
        for oc in range(NC):
            par = oc & 1
            def fhh(t, oc=oc, par=par):
                mi = None
                for s4 in range(4):
                    mi = t.matmul(pS1[:, 512 * s4:512 * (s4 + 1)],
                                  sW1F[:, 128 * oc:128 * (oc + 1)],
                                  sCT[:, 512 * s4:512 * (s4 + 1)],
                                  start=True, stop=True)
                return mi
            w = []
            if oc == 0:
                w.append(tanh_evc)
                w.append(ldg_w)
            if oc >= 1:
                w.append(relu_done[oc - 1])
            P.st("t", fhh, wait=w)
            a_hh = P.mark("t", "AUX")
            P.st("s", lambda s, oc=oc, par=par:
                 s.activation(sHH[:, par, :], pS1[:], AF.Relu, bias=sBIAS[:, 14 + oc:15 + oc]),
                 wait=[a_hh])
            hh_evc = P.mark("s", "EVC")
            relu_done.append(hh_evc)
            def fmag(t, oc=oc, par=par):
                mi = None
                for s4 in range(4):
                    mi = t.matmul(pM[:, 512 * s4:512 * (s4 + 1)],
                                  sW2BD[:, oc, :], sHH[:, par, 512 * s4:512 * (s4 + 1)],
                                  start=(oc == 0), stop=(oc == NC - 1))
                return mi
            P.st("t", fmag, wait=[hh_evc])
            mag_done.append(P.mark("t", "AUX3"))
        P.st("s", lambda s: s.activation(sMAGT[:], pM[:], AF.Identity, bias=sB2SP[:]),
             wait=[mag_done[-1]])
        mag_evc = P.mark("s", "EVC")
        s2es.close()
        if _fcut <= 11:
            P.barrier()
            fes.close()
            return

        t2es = ExitStack()
        pTB = psum(t2es, "pTB", [128, 2, 8, 128], BF16)
        tb_evc = []
        gi = 0
        for src, dst in ((sMAGT, sMGm), (sDOT, sDTm)):
            for g in range(2):
                par = gi & 1
                for k in range(8):
                    t = 8 * g + k
                    w = [tb_evc[gi - 2]] if (gi >= 2 and k == 0) else []
                    if gi == 0 and k == 0:
                        w.append(mag_evc)
                    P.st("t", lambda tt, t=t, par=par, k=k, src=src:
                         tt.transpose(pTB[:, par, k, 0:64], src[:, 128 * t:128 * (t + 1)],
                                      sIDB[0:64, 0:64]), wait=w)
                trid = P.mark("t", "TRID")
                P.st("v", lambda v, g=g, par=par, dst=dst:
                     v.tensor_copy(dst[:, 8 * g:8 * (g + 1), :], pTB[:, par, :, 0:64]),
                     wait=[trid])
                tb_evc.append(P.mark("v", "EVC"))
                gi += 1
        t2es.close()
        if _fcut <= 12:
            P.barrier()
            fes.close()
            return

        def wsel(v):
            for c in range(8):
                v.tensor_tensor(sTMP8[:], sLOH[:], bcast0(sCOH[:, :, c:c + 1], 2, 8), ALU.mult)
                v.tensor_tensor(sWSEL[:, :, 8 * c:8 * (c + 1)], sTMP8[:],
                                sMGm[:, :, 8 * c:8 * (c + 1)], ALU.mult)
            wf = sWSEL[:].rearrange("p t s -> p (t s)")
            return v.tensor_scalar(wf, wf, sKN[:, 17:18], None, ALU.mult)
        P.st("v", wsel, wait=[sel_aux, tb_evc[-1]])
        wsel_aux = P.mark("v", "AUX")
        if _fcut <= 13:
            P.barrier()
            fes.close()
            return

        nm2rs = sYS[:, :, 0:1]
        mu_y = sYS[:, :, 0:1].rearrange("p t o -> p (t o)")
        cross = sYS[:, :, 1:2].rearrange("p t o -> p (t o)")
        ffnsq = sYS[:, :, 2:3].rearrange("p t o -> p (t o)")
        exf2 = sYS[:, :, 3:4].rearrange("p t o -> p (t o)")
        rs2 = sRS2[:, :, 0:1].rearrange("p t o -> p (t o)")
        nm2 = sRS2[:, :, 1:2].rearrange("p t o -> p (t o)")
        rs1v = sRS[:, :, 0:1].rearrange("p t o -> p (t o)")
        nm1v = sRS[:, :, 1:2].rearrange("p t o -> p (t o)")
        sc0 = sSCR[:, :, 0:1].rearrange("p t o -> p (t o)")
        sc1b = sSCR[:, :, 1:2].rearrange("p t o -> p (t o)")

        def ystats(v):
            v.tensor_tensor(sSC64[:], sWSEL[:], bcast0(sDSB[:, 0:1, :], 1, NT), ALU.mult)
            v.reduce_sum(sYS[:, :, 0], sSC64[:], axis=AX.X)
            v.tensor_tensor(sSC64[:], sWSEL[:], sDTm[:], ALU.mult)
            v.reduce_sum(sYS[:, :, 1], sSC64[:], axis=AX.X)
            v.tensor_tensor(sSC64[:], sWSEL[:], sWSEL[:], ALU.mult)
            v.tensor_tensor(sSC64[:], sSC64[:], bcast0(sDSB[:, 1:2, :], 1, NT), ALU.mult)
            v.reduce_sum(sYS[:, :, 2], sSC64[:], axis=AX.X)
            v.tensor_tensor(exf2, rs1v, rs1v, ALU.mult)
            v.tensor_scalar(exf2, exf2, -EPS, 1.0, ALU.mult, ALU.add)
            v.tensor_scalar(mu_y, mu_y, 1.0 / 1024.0, None, ALU.mult)
            v.tensor_scalar(sc0, cross, 2.0 / 1024.0, None, ALU.mult)
            v.scalar_tensor_tensor(sc0, ffnsq, 1.0 / 1024.0, sc0, ALU.mult, ALU.add)
            v.tensor_tensor(sc0, sc0, exf2, ALU.add)
            v.tensor_tensor(sc1b, mu_y, mu_y, ALU.mult)
            v.tensor_tensor(sc1b, sc0, sc1b, ALU.subtract)
            v.tensor_scalar(sc1b, sc1b, EPS, None, ALU.add)
            return v.reciprocal(sc0, sc1b)
        P.st("v", ystats, wait=[wsel_aux])
        a_ys = P.mark("v", "AUX")
        ws_mu = sWS[:].rearrange("p (t k) -> p k t", k=2)[:, 0, :]
        ws_sig = sWS[:].rearrange("p (t k) -> p k t", k=2)[:, 1, :]

        def sig_s(s):
            s.activation(rs2, sc0, AF.Sqrt)
            return s.activation(ws_sig, sc1b, AF.Sqrt)
        P.st("s", sig_s, wait=[a_ys])
        a_ys2 = P.mark("s", "AUX")

        def newton2(v):
            v.tensor_tensor(sc0, rs2, rs2, ALU.mult)
            v.tensor_tensor(sc0, sc0, sc1b, ALU.mult)
            v.tensor_scalar(sc0, sc0, -0.5, 1.5, ALU.mult, ALU.add)
            v.tensor_tensor(rs2, rs2, sc0, ALU.mult)
            v.tensor_scalar(ws_mu, mu_y, -1.0, None, ALU.mult)
            return v.tensor_copy(sWSB[:], sWS[:])
        P.st("v", newton2, wait=[a_ys2])
        P.mark("v", "AUX")
        P.barrier()
        if _fcut <= 14:
            fes.close()
            return

        t3es = ExitStack()
        pTW = psum(t3es, "pTW", [64, 2, 8, 128], BF16)
        pRT = psum(t3es, "pRT", [2, 16, 128], BF16)
        for g in range(2):
            par = g & 1
            for k in range(8):
                t = 8 * g + k
                P.st("t", lambda tt, t=t, par=par, k=k:
                     tt.transpose(pTW[:, par, k, :], sWSEL[:, t, :], sIDB[:]))
            trid = P.mark("t", "TRID")
            P.st("v", lambda v, g=g, par=par:
                 v.tensor_copy(sWT[:, 1024 * g:1024 * (g + 1)].rearrange("p (k q) -> p k q", k=8),
                               pTW[:, par]),
                 wait=[trid])
            P.mark("v", "EVC")
        for t in range(NT):
            P.st("t", lambda tt, t=t:
                 tt.transpose(pRT[:, t, :], sWSB[:, 2 * t:2 * t + 2], sIDB[:]))
        rt_tr = P.mark("t", "TRID")
        P.st("v", lambda v: v.tensor_copy(sRT[:], pRT[:]), wait=[rt_tr])
        P.mark("v", "EVC")
        P.barrier()
        t3es.close()
        if _fcut <= 15:
            P.barrier()
            fes.close()
            return

        d2es = ExitStack()
        pD = psum(d2es, "pD", [128, 2, D], F32)
        ztail_hist = []
        for t in range(NT):
            par = t & 1
            def fdir(tt, t=t, par=par):
                for s2 in range(2):
                    tt.matmul(pD[:, par, 512 * s2:512 * (s2 + 1)],
                              sWT[:, 128 * t:128 * (t + 1)],
                              sDIRG[:, 512 * s2:512 * (s2 + 1)],
                              start=True, stop=False)
                for s2 in range(2):
                    tt.matmul(pD[:, par, 512 * s2:512 * (s2 + 1)],
                              sRT[:, t, :],
                              sGBr[:, 512 * s2:512 * (s2 + 1)],
                              start=False, stop=False)
                mi = None
                for c in range(NC):
                    mi = tt.matmul(pD[:, par, 128 * c:128 * (c + 1)],
                                   sT[:, c, 128 * t:128 * (t + 1)],
                                   sDG[:, c, :],
                                   start=False, stop=(c % 4 == 3))
                return mi
            w = [ldgb] if t == 0 else []
            if t >= 2:
                w.append(ztail_hist[t - 2])
            P.st("t", fdir, wait=w)
            a_t = P.mark("t", "AUX")

            P.st("v", lambda v, t=t, par=par:
                 v.scalar_tensor_tensor(sX[:, t, :], pD[:, par, :], sRS2[:, t, 0:1],
                                        sX[:, t, :], ALU.mult, ALU.add),
                 wait=[a_t])
            ztail_hist.append(P.mark("v", "EVC"))
            if post_stats:
                P.st("v", lambda v, t=t: v.bn_stats(sSTAT[:, t, 0:6], sX[:, t, 0:512]),
                     wait=[ztail_hist[-1]])
                P.st("v", lambda v, t=t: v.bn_stats(sSTAT[:, t, 6:12], sX[:, t, 512:1024]))
            if store_out:
                P.st("y", lambda e, t=t: e.dma_start(
                    out=dy.rearrange("(i p) f -> p i f", p=128)[:, t, :],
                    in_=sX[:, t, :]),
                    wait=[ztail_hist[-1]], inc=[("ST", 16)])
        if post_stats:
            bn_mark = P.mark("v", "AUX")
            for t in range(NT):
                P.st("v", lambda v, t=t: v.bn_aggr(sMV4[:, t, 0:2], sSTAT[:, t, 0:6]),
                     wait=[bn_mark] if t == 0 else [])
                P.st("v", lambda v, t=t: v.bn_aggr(sMV4[:, t, 2:4], sSTAT[:, t, 6:12]))
            post_ag[0] = P.mark("v", "AUX")
        P.barrier()
        d2es.close()
        fes.close()

    post_ag = [None]

    def emit_attn():
        import os as _os
        _acut = int(_os.environ.get("KERN_ATTCUT", "99"))
        if post_ag[0] is not None:
            aux = emit_ln_stats_tail(sRS, [post_ag[0]])
        else:
            aux = emit_ln_stats(lambda t: sX[:, t, :], sRS)
        ln2es = ExitStack()
        lnv2 = emit_ln_transpose(ln2es, lambda t: sX[:, t, :], sRS, aux)
        ln2es.close()

        if _acut <= 1:
            return
        aes = ExitStack()
        sBETC = sbuf(aes, "sBETC", [128, 8], BF16, side="right")
        sBOW = sbuf(aes, "sBOW", [1, D], BF16, side="right")
        sONE1B = sbuf(aes, "sONE1B", [1, 128], BF16, side="right")
        sNB = sbuf(aes, "sNB", [128, 8], F32)
        sMX = sbuf(aes, "sMX", [128, 12], F32)

        qk_es = ExitStack()
        sQT = sbuf(qk_es, "sQT", [128, NC, N], BF16)
        sKT = sbuf(qk_es, "sKT", [128, 2, N], BF16)
        sVTM = sbuf(qk_es, "sVTM", [128, NT, 4, 65], BF16)

        qkv_es = ExitStack()
        sVT = sbuf(qkv_es, "sVT", [64, 4, N], BF16)
        sWQ = sbuf(qkv_es, "sWQ", [128, NC, D], BF16)
        sWK = sbuf(qkv_es, "sWK", [128, NC, 256], BF16)
        sWV = sbuf(qkv_es, "sWV", [128, NC, 256], BF16)
        dma("g", sWQ[:], d_wq.rearrange("(c p) n -> p c n", p=128), sem="LGQ")
        dma("g", sWK[:], d_wk.rearrange("(c p) n -> p c n", p=128), sem="LGQ")
        dma("g", sWV[:], d_wv.rearrange("(c p) n -> p c n", p=128), sem="LGQ")
        ldg_qkv = ("LGQ", P.val["LGQ"])

        pQ_es = ExitStack()
        pQ = psum(pQ_es, "pQ", [128, 2, N], F32)
        jobs = ([("q", oc) for oc in range(NC)] +
                [("k", 0), ("k", 1), ("v", 0), ("v", 1)])
        wtab = {"q": sWQ, "k": sWK, "v": sWV}
        evac_hist = []
        for ji, (kind, oc) in enumerate(jobs):
            par = ji & 1
            def fproj(t, kind=kind, oc=oc, par=par):
                wt = wtab[kind]
                mi = None
                for s4 in range(4):
                    for c in range(NC):
                        mi = t.matmul(pQ[:, par, 512 * s4:512 * (s4 + 1)],
                                      wt[:, c, 128 * oc:128 * (oc + 1)],
                                      sT[:, c, 512 * s4:512 * (s4 + 1)],
                                      start=(c == 0), stop=(c == NC - 1))
                return mi
            w = [ldg_qkv, lnv2[NT - 1]] if ji == 0 else []
            if ji >= 2:
                w.append(evac_hist[ji - 2])
            P.st("t", fproj, wait=w)
            a_j = P.mark("t", "AUX")
            if kind == "v":
                def vevac(s, oc=oc, par=par):
                    s.activation(sVT[0:64, 2 * oc, :], pQ[0:64, par, :], AF.Identity,
                                 bias=sBIAS[0:64, 10 + oc:11 + oc], scale=1.0)
                    return s.activation(sVT[0:64, 2 * oc + 1, :], pQ[64:128, par, :], AF.Identity,
                                        bias=sBIAS[64:128, 10 + oc:11 + oc], scale=1.0)
                P.st("s", vevac, wait=[a_j])
            else:
                if kind == "q":
                    dst, bias, scale = sQT[:, oc, :], sBIAS[:, oc:oc + 1], 0.125
                else:
                    dst, bias, scale = sKT[:, oc, :], sBIAS[:, 8 + oc:9 + oc], 1.0
                P.st("s", lambda s, dst=dst, bias=bias, scale=scale, par=par:
                     s.activation(dst, pQ[:, par, :], AF.Identity, bias=bias, scale=scale),
                     wait=[a_j])
            evac_hist.append(P.mark("s", "EVC"))
        P.barrier()
        pQ_es.close()
        if _acut <= 2:
            qkv_es.close()
            qk_es.close()
            aes.close()
            return

        vt_es = ExitStack()
        pTB2 = psum(vt_es, "pTB2", [128, 2, 8, 128], BF16)
        P.st("v", lambda v: v.memset(sVTM[:], 1.0))
        vtm_ms = P.mark("v", "AUX")
        vtm_evc = []
        for mg in range(8):
            par = mg & 1
            for k in range(8):
                mloc = mg * 2 + k // 4
                g = k % 4
                w = [vtm_evc[mg - 2]] if (mg >= 2 and k == 0) else []
                P.st("t", lambda tt, mloc=mloc, g=g, par=par, k=k:
                     tt.transpose(pTB2[:, par, k, 0:64],
                                  sVT[0:64, g, 128 * mloc:128 * (mloc + 1)],
                                  sIDB[0:64, 0:64]), wait=w)
            trid = P.mark("t", "TRID")
            P.st("v", lambda v, mg=mg, par=par:
                 v.tensor_copy(sVTM[:, 2 * mg:2 * mg + 2, :, 0:64],
                               pTB2[:, par, :, 0:64].rearrange("p (m g) d -> p m g d", g=4)),
                 wait=[trid] + ([vtm_ms] if mg == 0 else []))
            vtm_evc.append(P.mark("v", "EVC"))
        P.barrier()
        vt_es.close()
        qkv_es.close()
        if _acut <= 3:
            qk_es.close()
            aes.close()
            return
        sATM = sbuf(aes, "sATM", [128, 8, 16, 65], BF16, side="right")
        sWO = sbuf(aes, "sWO", [128, NC, D], BF16, side="right")
        dma("g", sWO[:], d_wo.rearrange("(c p) n -> p c n", p=128), sem="LGO")
        dma("g", sBETC[:], d_betf.rearrange("(c p) -> p c", p=128), sem="LGO",
            allow_slow_non_contiguous=True)
        ldg_wo = ("LGO", P.val["LGO"])

        bb_es = ExitStack()
        sSQB = sbuf(bb_es, "sSQB", [128, 2, N], BF16)
        pBB = psum(bb_es, "pBB", [128, 2, N], F32)
        red_hist = []
        for ci in range(10):
            par = ci & 1
            src = sQT[:, ci, :] if ci < 8 else sKT[:, ci - 8, :]
            w = []
            if ci >= 2:
                w.append(red_hist[ci - 2])
            P.st("s", lambda s, src=src, par=par: s.activation(sSQB[:, par, :], src, AF.Square),
                 wait=w)
            a_sq = P.mark("s", "AUX")
            def fbb(t, par=par):
                mi = None
                for s4 in range(4):
                    mi = t.matmul(pBB[:, par, 512 * s4:512 * (s4 + 1)], sONESQB[:],
                                  sSQB[:, par, 512 * s4:512 * (s4 + 1)],
                                  start=True, stop=True)
                return mi
            P.st("t", fbb, wait=[a_sq])
            a_bb = P.mark("t", "AUX")
            P.st("v", lambda v, ci=ci, par=par:
                 v.reduce_max(sMX[:, ci:ci + 1],
                              pBB[:, par, :].rearrange("p (o f) -> p o f", o=1), axis=AX.X),
                 wait=[a_bb])
            red_hist.append(P.mark("v", "AUX3"))

        P.st("v", lambda v: v.tensor_tensor(
                 sNB[:].rearrange("p (a b) -> p a b", b=2),
                 sMX[:, 0:8].rearrange("p (a b) -> p a b", b=2),
                 bcast0(sMX[:, 8:10].rearrange("p (o b) -> p o b", o=1), 1, 4), ALU.mult),
             wait=[red_hist[-1]])
        a_bc = P.mark("v", "AUX")
        P.st("s", lambda s: s.activation(sNB[:], sNB[:], AF.Sqrt, scale=1.1025),
             wait=[a_bc])
        a_bs = P.mark("s", "AUX")
        P.st("v", lambda v: v.tensor_scalar(sNB[:], sNB[:], -1.0, None, ALU.mult),
             wait=[a_bs])
        P.mark("v", "AUX")
        P.barrier()
        bb_es.close()
        pbw_es = ExitStack()
        pBW = psum(pbw_es, "pBW", [1, 512], F32)
        dma("g", sBOW[:], d_bo, sem="LDBO")
        ld_bo = ("LDBO", P.val["LDBO"])
        P.st("v", lambda v: v.memset(sONE1B[:], 1.0))
        a_ms = P.mark("v", "AUX")

        bw_half = []
        for s2 in range(2):
            def fbw(t, s2=s2):
                mi = None
                for c in range(NC):
                    mi = t.matmul(pBW[0:1, :],
                                  sBETC[:, c:c + 1],
                                  sWO[:, c, 512 * s2:512 * (s2 + 1)],
                                  start=(c == 0), stop=(c == NC - 1))
                return mi
            w = [ldg_wo] if s2 == 0 else [bw_half[0]]
            P.st("t", fbw, wait=w)
            a_bw = P.mark("t", "AUX")
            wv = [a_bw, ld_bo, a_ms] if s2 == 0 else [a_bw]
            P.st("v", lambda v, s2=s2:
                 v.tensor_tensor(sBOW[:, 512 * s2:512 * (s2 + 1)],
                                 sBOW[:, 512 * s2:512 * (s2 + 1)], pBW[0:1, :], ALU.add),
                 wait=wv)
            bw_half.append(P.mark("v", "AUX"))
        a_bw2 = bw_half[-1]
        for c in range(NC):
            P.st("v", lambda v, c=c:
                 v.tensor_scalar(sWO[:, c, :], sWO[:, c, :], sBIAS[:, 22 + c:23 + c],
                                 None, ALU.mult),
                 wait=[a_bw2] if c == 0 else [])
        a_wo = P.mark("v", "AUX")
        if _acut <= 4:
            sc_dummy = None
            qk_es.close()
            aes.close()
            return

        sc_es = ExitStack()
        pS = psum(sc_es, "pS", [128, 2, 2, 512], F32)
        pO = psum(sc_es, "pO", [65, 2, 512], F32)
        ptrA = psum(sc_es, "ptrA", [128, 8, 66], BF16)
        sE = sbuf(sc_es, "sE", [128, 4, 2, 512], BF16)
        sTMPA = sbuf(sc_es, "sTMPA", [65, 2, 512], BF16)
        sRC = sbuf(sc_es, "sRC", [128, 4, 16], F32)
        sANBv = sT[:].rearrange("p c (t q) -> p t c q", t=NT)

        prev_atm_evc = 0
        prev_po_evc = 0
        prev_trid_tmpa = 0
        pending_evac = []
        pending_bnd = []
        since_evac = [99]
        prev_sq = [0]
        prev_t1 = [0]
        prev_rr = [0]
        norm_t1 = {}
        pending_sqj = []

        def emit_sq_flush():
            while pending_sqj:
                ts4, t1m = pending_sqj.pop(0)
                for i, t in enumerate(ts4):
                    w = []
                    if i == 0:
                        w = [t1m]
                        if prev_sq[0]:
                            w.append(prev_sq[0])
                    if i == 2:
                        w = [P.mark("s", "SQ")]
                    P.st("s", lambda s, t=t, slot=i % 2:
                         s.activation(sXFB[:, slot, :].rearrange("p (c q) -> p c q", q=128),
                                      sANBv[:, t], AF.Square,
                                      accum_out=sMV4[:, t, 1:2]),
                         wait=w)
                prev_sq[0] = P.mark("s", "SQ")

        def emit_norm_j(J, am):
            ts4 = list(range(4 * J, 4 * (J + 1)))
            for i, t in enumerate(ts4):
                w = [am] if i == 0 else []
                if i == 0 and prev_t1[0]:
                    w.append(prev_t1[0])
                P.st("v", lambda v, t=t:
                     v.reciprocal(sRC[:, t % 4, :],
                                  sATM[:, t % 8, :, 64:65].rearrange("p h o -> p (h o)")),
                     wait=w)
            rcm = P.mark("v", "AUX3")
            emit_sq_flush()
            for i, t in enumerate(ts4):
                P.st("v", lambda v, t=t:
                     v.tensor_tensor(
                         sANBv[:, t].rearrange("p c (hh d) -> p c hh d", hh=2),
                         sATM[:, t % 8, :, 0:64].rearrange("p (c hh) d -> p c hh d", hh=2),
                         bcast0(sRC[:, t % 4, :].rearrange("p (c hh o) -> p c hh o",
                                                           hh=2, o=1), 3, 64),
                         ALU.mult),
                     wait=[rcm] if i == 0 else [])
            prev_t1[0] = P.mark("v", "AUX3")
            norm_t1[J] = prev_t1[0]
            for i, t in enumerate(ts4):
                P.st("v", lambda v, t=t:
                     v.reduce_sum(sSTAT[:, t, 0:8], sANBv[:, t], axis=AX.X),
                     wait=[prev_t1[0]] if i == 0 else [])
            r1m = P.mark("v", "AUX3")
            for i, t in enumerate(ts4):
                P.st("v", lambda v, t=t:
                     v.reduce_sum(sMV4[:, t, 0:1],
                                  sSTAT[:, t, 0:8].rearrange("p (o c) -> p o c", o=1),
                                  axis=AX.X),
                     wait=[r1m] if i == 0 else [])
            prev_rr[0] = P.mark("v", "AUX3")
            pending_sqj.append((ts4, prev_t1[0]))
        exp_hist = {}
        pv_hist = {}
        gidx = 0
        lv_prev = [0]
        pend = []

        def emit_pv(args):
            g, hp, c, epar, lv, cmax, first = args
            wpv = [exp_hist[g]]
            if first and prev_po_evc:
                wpv.append(prev_po_evc)
            assert not (first and pending_evac), "first-of-J PV emitted before prior evac"
            def fpv(t, hp=hp, c=c, epar=epar, lv=lv, cmax=cmax):
                mi = None
                for h in range(2):
                    gg = (2 * hp + h) % 4
                    mi = t.matmul(pO[:, h, lv:512],
                                  sVTM[:, c, gg, :],
                                  sE[:, epar, h, lv:512],
                                  start=(c == 0), stop=(c == cmax - 1))
                return mi
            P.st("t", fpv, wait=wpv)
            pv_hist[g] = P.mark("t", "PVD")

        def emit_evac(last_g, bhp, bJ):
            nonlocal prev_po_evc
            for h in range(2):
                w = [pv_hist[last_g]]
                if h == 0 and prev_trid_tmpa:
                    w.append(prev_trid_tmpa)
                P.st("v", lambda v, h=h: v.tensor_copy(sTMPA[:, h, :], pO[:, h, :]),
                     wait=w)
            prev_po_evc = P.mark("v", "AUX")

            def boundary(hp=bhp, J=bJ, po_evc=prev_po_evc, atm_evc=prev_atm_evc):
                for tb in range(4):
                    for h in range(2):
                        w = [po_evc]
                        if tb == 0 and h == 0 and atm_evc:
                            w.append(atm_evc)
                        P.st("t", lambda tt, tb=tb, h=h:
                             tt.transpose(ptrA[:, tb * 2 + h, 0:65],
                                          sTMPA[:, h, 128 * tb:128 * (tb + 1)],
                                          sIDB[0:65, 0:65]), wait=w)
                trid = P.mark("t", "TRID")
                wat = [trid]
                if J >= 2 and (J - 2) in norm_t1:
                    wat.append(norm_t1[J - 2])
                P.st("v", lambda v, hp=hp, J=J:
                     v.tensor_copy(sATM[:, (J % 2) * 4:(J % 2) * 4 + 4,
                                        2 * hp:2 * hp + 2, :],
                                   ptrA[:, :, 0:65].rearrange("p (tb h) d -> p tb h d", h=2)),
                     wait=wat)
                am = P.mark("v", "AUX3")
                if hp == 7:
                    emit_norm_j(J, am)
                return trid, am
            pending_bnd.append(boundary)
            since_evac[0] = 0

        for J in range(4):
            cmax = 4 * J + 4
            for hp in range(8):
                kchunk = hp & 1
                for c in range(cmax):
                    g = gidx
                    gidx += 1
                    cpar = g & 1
                    epar = g & 3
                    lv = max(0, 128 * c - 512 * J)
                    q0 = 512 * J + lv
                    qe = 512 * J + 512
                    diag = (c >= 4 * J)
                    wqk = [exp_hist[g - 2]] if g >= 2 else []
                    def fqk(t, hp=hp, kchunk=kchunk, c=c, cpar=cpar, lv=lv, q0=q0, qe=qe):
                        mi = None
                        for h in range(2):
                            o = 64 * h
                            mi = t.matmul(pS[:, cpar, h, lv:512],
                                          sKT[o:o + 64, kchunk, 128 * c:128 * (c + 1)],
                                          sQT[o:o + 64, hp, q0:qe], start=True, stop=True,
                                          tile_position=(64 * h, 0))
                        return mi
                    P.st("t", fqk, wait=wqk)
                    qkd = P.mark("t", "QKD")
                    pend.append((g, hp, c, epar, lv, cmax, c == 0))
                    if len(pend) > 2:
                        emit_pv(pend.pop(0))
                    since_evac[0] += 1
                    if pending_bnd and since_evac[0] >= 2:
                        prev_trid_tmpa, prev_atm_evc = pending_bnd.pop(0)()
                    elif pending_evac and not pending_bnd and pending_evac[0][0] in pv_hist:
                        emit_evac(*pending_evac.pop(0))
                    wexp = [qkd]
                    if g >= 4:
                        wexp.append(pv_hist[g - 4])
                    P.st("s", lambda s, cpar=cpar, epar=epar, lv=lv, hp=hp:
                         s.activation(sE[:, epar, :, lv:512], pS[:, cpar, :, lv:512],
                                      AF.Exp, bias=sNB[:, hp:hp + 1]),
                         wait=wexp)
                    exp_hist[g] = P.mark("s", "EXPD")
                    if diag:
                        P.st("v", lambda v, epar=epar, lv=lv:
                             v.tensor_tensor(sE[:, epar, :, lv:lv + 128],
                                             sE[:, epar, :, lv:lv + 128], sTRI[:], ALU.mult),
                             wait=[exp_hist[g]])
                        exp_hist[g] = P.mark("v", "TRID")
                pending_evac.append((gidx - 1, hp, J))
        while pend:
            emit_pv(pend.pop(0))
        while pending_evac or pending_bnd:
            if pending_bnd:
                prev_trid_tmpa, prev_atm_evc = pending_bnd.pop(0)()
            else:
                emit_evac(*pending_evac.pop(0))
        emit_sq_flush()
        mu3 = sSCR[:, :, 2:3].rearrange("p t o -> p (t o)")
        c30 = sSCR[:, :, 0:1].rearrange("p t o -> p (t o)")
        c31 = sSCR[:, :, 1:2].rearrange("p t o -> p (t o)")
        s1f = sMV4[:, :, 0:1].rearrange("p t o -> p (t o)")
        s2f = sMV4[:, :, 1:2].rearrange("p t o -> p (t o)")
        rs3 = sRS[:, :, 0:1].rearrange("p t o -> p (t o)")
        nm3 = sRS[:, :, 1:2].rearrange("p t o -> p (t o)")

        def stats3(v):
            v.tensor_scalar(mu3, s1f, 1.0 / 1024.0, None, ALU.mult)
            v.tensor_tensor(c30, mu3, mu3, ALU.mult)
            v.tensor_scalar(c31, s2f, 1.0 / 1024.0, None, ALU.mult)
            v.tensor_tensor(c31, c31, c30, ALU.subtract)
            v.tensor_scalar(c31, c31, EPS, None, ALU.add)
            return v.reciprocal(c30, c31)
        P.st("v", stats3, wait=[prev_sq[0], prev_rr[0]])
        a30 = P.mark("v", "AUX")
        P.st("s", lambda s: s.activation(rs3, c30, AF.Sqrt), wait=[a30])
        a31 = P.mark("s", "AUX")

        def newton3(v):
            v.tensor_tensor(c30, rs3, rs3, ALU.mult)
            v.tensor_tensor(c30, c30, c31, ALU.mult)
            v.tensor_scalar(c30, c30, -0.5, 1.5, ALU.mult, ALU.add)
            v.tensor_tensor(rs3, rs3, c30, ALU.mult)
            v.tensor_tensor(nm3, mu3, rs3, ALU.mult)
            return v.tensor_scalar(nm3, nm3, -1.0, None, ALU.mult)
        P.st("v", newton3, wait=[a31])
        aux3 = P.mark("v", "AUX")
        P.barrier()
        sc_es.close()
        pbw_es.close()
        qk_es.close()
        if _acut <= 6:
            aes.close()
            return
        ln3es = ExitStack()
        lnv3 = emit_ln_transpose(ln3es, lambda t: sANBv[:, t], sRS,
                                 [aux3, prev_sq[0]], src3d=True)

        pw_es = ExitStack()
        pW = psum(pw_es, "pW", [128, 2, D], F32)
        res_hist = []
        for t in range(NT):
            par = t & 1
            def fwo(tt, t=t, par=par):
                mi = None
                for s2 in range(2):
                    tt.matmul(pW[:, par, 512 * s2:512 * (s2 + 1)], sONE1B[:],
                              sBOW[0:1, 512 * s2:512 * (s2 + 1)], start=True, stop=False)
                    for c in range(NC):
                        mi = tt.matmul(pW[:, par, 512 * s2:512 * (s2 + 1)],
                                       sT[:, c, 128 * t:128 * (t + 1)],
                                       sWO[:, c, 512 * s2:512 * (s2 + 1)],
                                       start=False, stop=(c == NC - 1))
                return mi
            w = [a_wo, lnv3[t]] if t == 0 else [lnv3[t]]
            if t >= 2:
                w.append(res_hist[t - 2])
            P.st("t", fwo, wait=w)
            a_t = P.mark("t", "AUX")
            P.st("v", lambda v, t=t, par=par:
                 v.tensor_tensor(sX[:, t, :], sX[:, t, :], pW[:, par, :], ALU.add),
                 wait=[a_t])
            res_hist.append(P.mark("v", "EVC"))
            P.st("v", lambda v, t=t: v.bn_stats(sSTAT[:, t, 0:6], sX[:, t, 0:512]),
                 wait=[res_hist[-1]])
            P.st("v", lambda v, t=t: v.bn_stats(sSTAT[:, t, 6:12], sX[:, t, 512:1024]))
        bn_mark = P.mark("v", "AUX")
        for t in range(NT):
            P.st("v", lambda v, t=t: v.bn_aggr(sMV4[:, t, 0:2], sSTAT[:, t, 0:6]),
                 wait=[bn_mark] if t == 0 else [])
            P.st("v", lambda v, t=t: v.bn_aggr(sMV4[:, t, 2:4], sSTAT[:, t, 6:12]))
        post_ag[0] = P.mark("v", "AUX")
        P.barrier()
        pw_es.close()
        ln3es.close()
        aes.close()

    import os as _os
    _nph = int(_os.environ.get("KERN_PHASES", "3"))
    _phases = [lambda: emit_ffn(first=True, post_stats=True), emit_attn,
               lambda: emit_ffn(pre_ag=post_ag[0], store_out=True)]
    for _f in _phases[:_nph]:
        _f()

    if _nph == 3:
        P.st("g", None, wait=[("ST", 256)])
    else:
        final_snap = list(P.last_snap)
        P.st("y", lambda e: e.dma_start(out=dy.rearrange("(i p) f -> p i f", p=128), in_=sX[:]),
             wait=final_snap,
             inc=[("ST", 16)])
        P.st("g", None, wait=[("ST", 16)])

    sems = P.sem
    steps = P.steps

    OPS = {"tensor_scalar", "tensor_tensor", "tensor_copy", "reciprocal",
           "reduce_max", "reduce_sum", "bn_stats", "bn_aggr",
           "scalar_tensor_tensor", "memset", "activation", "square",
           "copy", "mul", "add", "sign", "tensor_single_scalar", "select"}

    import os as _os2
    _DRAINS = _os2.environ.get("KERN_NODRAIN", "") != "1"

    class DrainProxy:
        def __init__(self, eng, sem, ctr):
            self._e = eng
            self._sem = sem
            self._ctr = ctr
            self._last = None

        def __getattr__(self, name):
            a = getattr(self._e, name)
            if name in OPS and _DRAINS:
                def wrapped(*args, **kw):
                    if self._last is not None:
                        self._ctr[0] += 1
                        self._last.then_inc(self._sem, 1)
                        self._e.wait_ge(self._sem, self._ctr[0])
                    self._last = a(*args, **kw)
                    return self._last
                return wrapped
            return a

    for s in ("SD_v", "SD_s"):
        P.new_sem(s)

    def run_for(eng_obj, tag):
        use_proxy = tag in ("v", "s")
        ctr = [0]
        for stp in steps:
            if stp["eng"] != tag:
                continue
            for (s, v) in stp["waits"]:
                eng_obj.wait_ge(sems[s], v)
            if stp["fn"] is not None:
                eo = DrainProxy(eng_obj, sems["SD_" + tag], ctr) if use_proxy else eng_obj
                instr = stp["fn"](eo)
                assert instr is not None, f"step on {tag} returned no instruction"
                for (s, n) in stp["incs"]:
                    instr.then_inc(sems[s], n)

    with es:
        with nc.Block() as block:
            @block.sync
            def _(sync):
                run_for(sync, "y")

            @block.tensor
            def _(tensor):
                run_for(tensor, "t")

            @block.vector
            def _(vector):
                run_for(vector, "v")

            @block.scalar
            def _(scalar):
                run_for(scalar, "s")

            @block.gpsimd
            def _(gpsimd):
                run_for(gpsimd, "g")

    return nc


_NC_CACHE = {}


def _get_nc():
    import os as _os
    key = "nc" + _os.environ.get("KERN_PHASES", "3") + "." + _os.environ.get("KERN_FFNCUT", "99") + "." + _os.environ.get("KERN_ATTCUT", "99")
    if key not in _NC_CACHE:
        _NC_CACHE[key] = build_program()
    return _NC_CACHE[key]


def kernel(**inputs):
    f32 = np.float32
    x = np.asarray(inputs["L0_attn_x"], f32)
    B = x.shape[0]
    sw1 = np.asarray(inputs["L0_ffn_spline_w1"], f32)
    sb1 = np.asarray(inputs["L0_ffn_spline_b1"], f32)
    sw2 = np.asarray(inputs["L0_ffn_spline_w2"], f32)
    w2bd = np.zeros((D, 64), f32)
    for tt in range(64):
        w2bd[tt * 16:(tt + 1) * 16, tt] = sw2[tt, :, 0]
    dirs = np.ascontiguousarray(np.asarray(inputs["L0_ffn_directions"], f32))

    tri = np.zeros((128, 128), f32)
    for m in range(128):
        tri[m, m:] = 1.0
    tri2 = np.ascontiguousarray(np.stack([tri, tri], axis=1)).astype(ml_dtypes.bfloat16)
    prio = np.ascontiguousarray(np.broadcast_to(np.arange(8, 0, -1, dtype=f32), (128, 8)))

    lnb = np.asarray(inputs["L0_attn_ln_b"], f32)
    common = dict(
        wq=np.ascontiguousarray(np.asarray(inputs["L0_attn_Wq"], f32)),
        wk=np.ascontiguousarray(np.asarray(inputs["L0_attn_Wk"], f32)),
        wv=np.ascontiguousarray(np.asarray(inputs["L0_attn_Wv"], f32)),
        wo=np.ascontiguousarray(np.asarray(inputs["L0_attn_Wo"], f32)),
        bq=np.ascontiguousarray(np.asarray(inputs["L0_attn_bq"], f32)),
        bk=np.ascontiguousarray(np.asarray(inputs["L0_attn_bk"], f32)),
        bv=np.ascontiguousarray(np.asarray(inputs["L0_attn_bv"], f32)),
        bo=np.ascontiguousarray(np.asarray(inputs["L0_attn_bo"], f32).reshape(1, D)),
        lng=np.ascontiguousarray(np.asarray(inputs["L0_attn_ln_g"], f32)),
        betf=np.ascontiguousarray(lnb.reshape(D)),
        nbet=np.ascontiguousarray(np.asarray(inputs["L0_ffn_norm_beta"], f32).reshape(1, D)),
        sigsT=np.ascontiguousarray(np.asarray(inputs["L0_ffn_signatures_raw"], f32).T),
        knots=np.ascontiguousarray(np.asarray(inputs["L0_ffn_knot_values"], f32).reshape(1, 8)),
        temp=np.ascontiguousarray(np.asarray(inputs["L0_ffn_temperature"], f32).reshape(1, 1)),
        cw1=np.ascontiguousarray(np.asarray(inputs["L0_ffn_compress_w1"], f32)),
        cb1=np.ascontiguousarray(np.asarray(inputs["L0_ffn_compress_b1"], f32)),
        cw2=np.ascontiguousarray(np.asarray(inputs["L0_ffn_compress_w2"], f32)),
        cb2=np.ascontiguousarray(np.asarray(inputs["L0_ffn_compress_b2"], f32)),
        w1f=np.ascontiguousarray(sw1.transpose(1, 0, 2).reshape(2, D)),
        b1f=np.ascontiguousarray(sb1.reshape(D)),
        w2bd=w2bd,
        b2sp=np.ascontiguousarray(np.asarray(inputs["L0_ffn_spline_b2"], f32)[:, 0]),
        dirs=dirs,
        dirsT=np.ascontiguousarray(dirs.T),
        gam=np.ascontiguousarray(np.asarray(inputs["L0_ffn_norm_gamma"], f32).reshape(1, D)),
        osc=np.ascontiguousarray(np.asarray(inputs["L0_ffn_output_scale"], f32).reshape(1, 1)),
        idf=np.eye(128, dtype=f32),
        idb=np.eye(128, dtype=f32).astype(ml_dtypes.bfloat16),
        tri=tri2,
        negi=(np.eye(128, dtype=f32) * -1e10).astype(ml_dtypes.bfloat16),
        tril=np.ascontiguousarray(1.0 - tri).astype(ml_dtypes.bfloat16),
        prio=prio,
        onesf=np.ones((1, 128), f32),
        onesqb=np.ones((128, 128), f32).astype(ml_dtypes.bfloat16),
    )

    nc = _get_nc()
    in_maps = []
    for b in range(B):
        m = dict(common)
        m["x"] = np.ascontiguousarray(x[b])
        in_maps.append(m)
    res = run_bass_kernel_spmd(nc, in_maps, core_ids=list(range(B)))
    out = np.stack([res.results[b]["y"] for b in range(B)], axis=0)
    return out.astype(np.float32)
